# version 13
# baseline (speedup 1.0000x reference)
import sys
if "/opt/trn_rl_repo" not in sys.path:
    sys.path.insert(0, "/opt/trn_rl_repo")
import math
import contextlib
import numpy as np
import ml_dtypes
import concourse.bass as bass
import concourse.mybir as mybir
from concourse.bass_utils import run_bass_kernel_spmd

F32 = mybir.dt.float32
BF16 = mybir.dt.bfloat16
AF = mybir.ActivationFunctionType
ALU = mybir.AluOpType
bfdt = ml_dtypes.bfloat16
f32 = np.float32

ENGINES = ("sync", "gpsimd", "tensor", "vector", "scalar")
RANKS = 4


def full_cfg():
    return dict(B=2, S=4096, D=4096, HD=128, NQH=32, NKVH=8, I=11008,
                theta=10000.0, eps=1e-5)


def derive(cfg):
    c = dict(cfg)
    c["HT"] = c["D"] // 128
    c["ST"] = c["S"] // 128
    c["SL"] = c["ST"] // RANKS
    c["OWN"] = c["SL"] * 128
    c["RB"] = min(512, c["OWN"])
    c["NRB"] = c["OWN"] // c["RB"]
    c["IT"] = c["I"] // 128
    nsb = 4
    base = c["IT"] // nsb
    rem = c["IT"] % nsb
    c["SBS"] = [base + (1 if i < rem else 0) for i in range(nsb)]
    c["T"] = [c["ST"] - RANKS * s for s in range(c["SL"])]
    c["GP"] = 4
    assert c["NQH"] == c["NKVH"] * c["GP"]
    assert c["HD"] == 128
    return c


def snake_pos(cfg, rank, s):
    base = cfg["ST"] - 1 - RANKS * s
    return base - rank if s % 2 == 0 else base - (RANKS - 1) + rank


class KB:
    def __init__(self):
        self.q = {e: [] for e in ENGINES}
        self.cnt = {}
        self.waited = {e: {} for e in ENGINES}
        self.last_real = {e: None for e in ENGINES}
        self.dma_sems = {e: set() for e in ENGINES}
        self.dma_since_bar = {e: False for e in ENGINES}
        self.sem_names = []

    def sem(self, name):
        if name not in self.cnt:
            self.cnt[name] = 0
            self.sem_names.append(name)
        return name

    def val(self, sem):
        return self.cnt.get(sem, 0)

    def op(self, eng, fn, inc=None):
        rec = {"fn": fn, "incs": []}
        if inc is not None:
            self.sem(inc)
            self.cnt[inc] += 1
            rec["incs"].append((inc, 1))
            rec["val"] = (inc, self.cnt[inc])
        self.q[eng].append(rec)
        self.last_real[eng] = rec
        return self.cnt[inc] if inc is not None else None

    def dma(self, eng, fn, sem):
        self.sem(sem)
        self.cnt[sem] += 16
        rec = {"fn": fn, "incs": [(sem, 16)]}
        self.q[eng].append(rec)
        self.dma_sems[eng].add(sem)
        self.dma_since_bar[eng] = True
        return self.cnt[sem]

    def wait(self, eng, sem, val):
        if sem is None or val is None or val <= 0:
            return
        w = self.waited[eng]
        if w.get(sem, 0) >= val:
            return
        w[sem] = val
        self.q[eng].append({"wait": (sem, val)})

    def barrier(self):
        ph = self.sem("ph")
        for e in ENGINES:
            if self.dma_since_bar[e]:
                for dsem in sorted(self.dma_sems[e]):
                    self.wait(e, dsem, self.cnt[dsem])
                self.cnt[ph] += 1
                self.q[e].append({"nopinc": (ph, 1)})
                self.dma_since_bar[e] = False
            if self.last_real[e] is not None:
                rec = self.last_real[e]
                if rec["incs"]:
                    s, v = rec["val"]
                    self.wait(e, s, v)
                    self.cnt[ph] += 1
                    self.q[e].append({"nopinc": (ph, 1)})
                else:
                    self.cnt[ph] += 1
                    rec["incs"].append((ph, 1))
                    rec["val"] = (ph, self.cnt[ph])
                self.last_real[e] = None
        tgt = self.cnt[ph]
        for e in ENGINES:
            self.wait(e, ph, tgt)

    def runner(self, eng, semobjs):
        lst = self.q[eng]

        def run(e):
            for rec in lst:
                if "wait" in rec:
                    s, v = rec["wait"]
                    e.wait_ge(semobjs[s], v)
                elif "nopinc" in rec:
                    s, n = rec["nopinc"]
                    e.nop().then_inc(semobjs[s], n)
                else:
                    ins = rec["fn"](e)
                    for s, n in rec["incs"]:
                        ins = ins.then_inc(semobjs[s], n)
        return run


def build_program(cfg, phases=99):
    c = derive(cfg)
    D, S, HT, ST, SL = c["D"], c["S"], c["HT"], c["ST"], c["SL"]
    OWN, RB, NRB = c["OWN"], c["RB"], c["NRB"]
    NQH, NKVH, GP, IT = c["NQH"], c["NKVH"], c["GP"], c["IT"]
    HD = 128
    NCH = 512
    nc = bass.Bass("TRN2", target_bir_lowering=False, debug=False)

    ap = lambda t: t.ap()
    x_own = ap(nc.dram_tensor("x_own", [OWN, D], F32, kind="ExternalInput"))
    x_full = ap(nc.dram_tensor("x_full", [S, D], F32, kind="ExternalInput"))
    NPASS = (NKVH + 1) // 2
    PW = min(2, NKVH) * 256
    w_q = ap(nc.dram_tensor("w_q", [NQH, 128, HT, 128], F32, kind="ExternalInput"))
    w_kv = ap(nc.dram_tensor("w_kv", [NPASS, 128, HT, PW], F32, kind="ExternalInput"))
    w_o = ap(nc.dram_tensor("w_o", [HT, 128, HT, 128], F32, kind="ExternalInput"))
    w_guT = ap(nc.dram_tensor("w_guT", [IT, 128, 2 * HT * 128], F32, kind="ExternalInput"))
    w_dT = ap(nc.dram_tensor("w_dT", [HT, 128, IT, 128], F32, kind="ExternalInput"))
    identb_d = ap(nc.dram_tensor("identb", [128, 128], BF16, kind="ExternalInput"))
    identf_d = ap(nc.dram_tensor("identf", [128, 128], F32, kind="ExternalInput"))
    onesc_d = ap(nc.dram_tensor("onesc", [128, 1], F32, kind="ExternalInput"))
    onesr_d = ap(nc.dram_tensor("onesr", [1, 128], F32, kind="ExternalInput"))
    onesb_d = ap(nc.dram_tensor("onesb", [128, 1], BF16, kind="ExternalInput"))
    cosd_d = ap(nc.dram_tensor("cosd", [128, S], F32, kind="ExternalInput"))
    sneg_d = ap(nc.dram_tensor("sneg", [64, S], F32, kind="ExternalInput"))
    spos_d = ap(nc.dram_tensor("spos", [64, S], F32, kind="ExternalInput"))
    cosq_d = ap(nc.dram_tensor("cosq", [128, OWN], F32, kind="ExternalInput"))
    snegq_d = ap(nc.dram_tensor("snegq", [64, OWN], F32, kind="ExternalInput"))
    sposq_d = ap(nc.dram_tensor("sposq", [64, OWN], F32, kind="ExternalInput"))
    masks_d = ap(nc.dram_tensor("masks", [SL * 4, 128, 512], BF16, kind="ExternalInput"))
    out_d = ap(nc.dram_tensor("out_own", [OWN, D], F32, kind="ExternalOutput"))

    xnT_own_d = ap(nc.dram_tensor("xnT_own_d", [HT, 128, OWN], BF16, kind="Internal"))
    xnT_full_d = ap(nc.dram_tensor("xnT_full_d", [HT, 128, S], BF16, kind="Internal"))
    q_d = ap(nc.dram_tensor("q_d", [NKVH, SL, GP, 128, 128], BF16, kind="Internal"))
    attnT_d = ap(nc.dram_tensor("attnT_d", [HT, 128, OWN], BF16, kind="Internal"))
    y_d = ap(nc.dram_tensor("y_d", [HT, 128, OWN], F32, kind="Internal"))


    kb = KB()
    es = contextlib.ExitStack()

    def sb(name, shape, dt):
        return es.enter_context(nc.sbuf_tensor(name, shape, dt))

    def ps(name, shape, dt):
        return es.enter_context(nc.psum_tensor(name, shape, dt))

    AR = sb("AR", [128, 65536], BF16)
    ARF = AR[:, :].bitcast(F32)
    U1 = sb("U1", [128, 16384], BF16)
    U1F = U1[:, :].bitcast(F32)
    U2 = sb("U2", [128, 8192], BF16)
    U2F = U2[:, :].bitcast(F32)
    U3 = sb("U3", [128, 6144], BF16)
    U3F = U3[:, :].bitcast(F32)
    identb = sb("identb_s", [128, 128], BF16)
    identf = sb("identf_s", [128, 128], F32)
    onesc = sb("onesc_s", [128, 1], F32)
    onesr = sb("onesr_s", [1, 128], F32)
    stat6 = [sb(f"stat6{i}", [128, 6 * (D // NCH)], F32) for i in range(2)]
    stat2 = [sb(f"stat2{i}", [128, 2], F32) for i in range(2)]
    rstd = [sb(f"rstd{i}", [128, 1], F32) for i in range(2)]
    negmr = [sb(f"negmr{i}", [128, 1], F32) for i in range(2)]
    recip = sb("recip", [1, 1024], F32)
    bcs = sb("bcs", [128, 512], F32)

    PM = ps("PM", [128, 2048], F32)
    pm = [PM[:, i * 512:(i + 1) * 512] for i in range(4)]
    ptr = ps("ptr", [128, 2048], BF16)
    ptrF = [ptr[:, 0:1024].bitcast(F32), ptr[:, 1024:2048].bitcast(F32)]
    pl = ps("pl", [128, 512], F32)
    pbc = ps("pbc", [128, 512], F32)

    SC = np.float32(1.0 / math.sqrt(HD))
    EB = np.float32(-16.0)

    for dst, src in ((identb, identb_d), (identf, identf_d), (onesc, onesc_d),
                     (onesr, onesr_d)):
        kb.dma("sync", (lambda d=dst, s=src: lambda e: e.dma_start(out=d[:, :], in_=s))(), "dC")
    vC = kb.val("dC")

    AEND = 32768
    xbuf = [U1F[:, 0:D], U1F[:, D:2 * D],
            ARF[:, AEND - 2 * D:AEND - D], ARF[:, AEND - D:AEND]]
    NXB = 4
    xnbuf = [U2[:, 0:D], U2[:, D:2 * D]]
    xnT_own = AR[:, 0:HT * OWN].rearrange("p (h r) -> p h r", h=HT)
    PMB = PM[:, :].bitcast(BF16)
    LN_SLOTS = [(ptr, 0), (ptr, 1024), (PMB, 0), (PMB, 1024), (PMB, 2048), (PMB, 3072)]
    kb._ptr_evac = {i: (("sES" if i % 2 == 0 else "vT"), 0) for i in range(6)}
    kb._ln_slot_rr = [0]

    def ln_pass(ntiles, src_of, dst_of, store_of, tag):
        nch = D // NCH
        state = {}
        xg = kb._lnp_xg if hasattr(kb, "_lnp_xg") else {i: ("sS", 0) for i in range(NXB)}
        tpg = kb._lnp_tpg if hasattr(kb, "_lnp_tpg") else {0: 0, 1: 0}
        for t in range(ntiles + 1):
            if t >= 1:
                u = t - 1
                par = u % 2
                xpar = u % NXB
                kb.wait("vector", "vS", state.pop(u))
                vagg = kb.op("vector",
                             (lambda par=par: lambda e: e.bn_aggr(stat2[par][:, :],
                                                                  stat6[par][:, :]))(),
                             inc="vS")
                kb.wait("scalar", "vS", vagg)
                s1 = kb.op("scalar",
                           (lambda par=par: lambda e: e.activation(
                               rstd[par][:, :], stat2[par][:, 1:2], AF.Ln,
                               bias=np.float32(c["eps"])))(), inc="sS")
                kb.wait("scalar", "sS", s1)
                s2 = kb.op("scalar",
                           (lambda par=par: lambda e: e.activation(
                               rstd[par][:, :], rstd[par][:, :], AF.Exp,
                               scale=np.float32(-0.5)))(), inc="sS")
                kb.wait("vector", "sS", s2)
                v2 = kb.op("vector",
                           (lambda par=par: lambda e: e.tensor_scalar(
                               negmr[par][:, :], stat2[par][:, 0:1],
                               rstd[par][:, 0:1], -1.0, ALU.mult, ALU.mult))(),
                           inc="vS")
                kb.wait("scalar", "vS", v2)
                kb.wait("scalar", "tS", tpg[par])
                hready = {}
                HHF = max(1, HT // 2)
                for hf in range(0, HT, HHF):
                    r = kb.op("scalar",
                              (lambda par=par, xpar=xpar, hf=hf, HHF=HHF: lambda e: e.activation(
                                  xnbuf[par][:, hf * 128:(hf + HHF) * 128],
                                  xbuf[xpar][:, hf * 128:(hf + HHF) * 128], AF.Identity,
                                  bias=negmr[par][:, 0:1],
                                  scale=rstd[par][:, 0:1]))(), inc="sS")
                    for h2 in range(hf, hf + HHF):
                        hready[h2] = r
                xg[xpar] = ("sS", r)
            if t < ntiles:
                xpar2 = t % NXB
                par2 = t % 2
                s_, v_ = xg[xpar2]
                kb.wait("sync", s_, v_)
                dv = kb.dma("sync",
                            (lambda t=t, xpar2=xpar2: lambda e: e.dma_start(
                                out=xbuf[xpar2][:, :], in_=src_of(t)))(),
                            f"dX{xpar2}")
                kb.wait("vector", f"dX{xpar2}", dv)
                for ch in range(nch):
                    kb.op("vector",
                          (lambda ch=ch, par2=par2, xpar2=xpar2: lambda e: e.bn_stats(
                              stat6[par2][:, ch * 6:(ch + 1) * 6],
                              xbuf[xpar2][:, ch * NCH:(ch + 1) * NCH]))(),
                          inc="vS" if ch == nch - 1 else None)
                state[t] = kb.val("vS")
            if t >= 1:
                GW = min(8, HT)
                dfn = dst_of(u)
                evs = {}
                for ht0 in range(0, HT, GW):
                    slot = kb._ln_slot_rr[0] % 6
                    kb._ln_slot_rr[0] += 1
                    sbase, soff = LN_SLOTS[slot]
                    esem, eval_ = kb._ptr_evac[slot]
                    kb.wait("tensor", esem, eval_)
                    kb.wait("tensor", "dC", vC)
                    for k in range(GW):
                        ht = ht0 + k
                        kb.wait("tensor", "sS", hready[ht])
                        kb.op("tensor",
                              (lambda ht=ht, k=k, par=par, sbase=sbase, soff=soff: lambda e: e.transpose(
                                  sbase[:, soff + k * 128: soff + (k + 1) * 128],
                                  xnbuf[par][:, ht * 128:(ht + 1) * 128], identb[:, :]))(),
                              inc="tS" if k == GW - 1 else None)
                    tv = kb.val("tS")
                    eng, sem = ("scalar", "sES") if slot % 2 == 0 else ("vector", "vT")
                    kb.wait(eng, "tS", tv)
                    ev = kb.op(eng,
                               (lambda dfn=dfn, ht0=ht0, eng=eng, GW=GW, sbase=sbase, soff=soff: lambda e: (
                                   e.copy if eng == "scalar" else e.tensor_copy)(
                                   dfn(ht0),
                                   sbase[:, soff: soff + GW * 128].rearrange(
                                       "p (a b) -> p a b", a=GW)))(),
                               inc=sem)
                    kb._ptr_evac[slot] = (sem, ev)
                    evs[sem] = ev
                tpg[par] = kb.val("tS")
                store_of(u, evs)
        kb._lnp_xg = xg
        kb._lnp_tpg = tpg

    def pa_store(u, evs):
        for sem, v in evs.items():
            kb.wait("gpsimd", sem, v)
        kb.dma("gpsimd",
               (lambda u=u: lambda e: e.dma_start(
                   out=xnT_own_d.rearrange("h p r -> p h r")[:, :, u * 128:(u + 1) * 128],
                   in_=xnT_own[:, :, u * 128:(u + 1) * 128]))(),
               "dXO")

    ln_pass(OWN // 128,
            lambda t: x_own[t * 128:(t + 1) * 128, :],
            lambda u: (lambda ht0: xnT_own[:, ht0:ht0 + min(8, HT), u * 128:(u + 1) * 128]),
            pa_store, "pa")

    nst = 4
    stg3 = [AR[:, 32768 + i * HT * 128: 32768 + (i + 1) * HT * 128]
            .rearrange("p (h r) -> p h r", h=HT) for i in range(nst)]
    kb._pb_st = {i: 0 for i in range(nst)}

    def pb_dst(u):
        st_ = u % nst
        kb.wait("scalar", f"dXF{st_}", kb._pb_st[st_])
        kb.wait("vector", f"dXF{st_}", kb._pb_st[st_])
        return lambda ht0: stg3[st_][:, ht0:ht0 + min(8, HT), :]

    def pb_store(u, evs):
        st_ = u % nst
        for sem, v in evs.items():
            kb.wait("gpsimd", sem, v)
        kb._pb_st[st_] = kb.dma("gpsimd",
               (lambda u=u, st_=st_: lambda e: e.dma_start(
                   out=xnT_full_d.rearrange("h p t -> p h t")[:, :, u * 128:(u + 1) * 128],
                   in_=stg3[st_][:, :, :]))(),
               f"dXF{st_}")

    ln_pass(ST,
            lambda t: x_full[t * 128:(t + 1) * 128, :],
            pb_dst, pb_store, "pb")
    kb.barrier()
    if phases <= 1:
        return _finish(nc, kb, es)

    wq = [U1[:, i * (HT * 128):(i + 1) * (HT * 128)] for i in range(2)]
    cosq = U2F[:, 0:OWN]
    snegq = U2F[0:64, OWN:OWN + OWN]
    sposq = U2F[0:64, 2 * OWN:3 * OWN]
    t1b = U3F[:, 0:512]
    t2b = U3F[:, 512:1024]
    qpiece = U3[:, 2048:2048 + OWN]

    kb.dma("sync", lambda e: e.dma_start(out=cosq[:, :], in_=cosq_d), "dT")
    kb.dma("sync", lambda e: e.dma_start(out=snegq[:, :], in_=snegq_d), "dT")
    kb.dma("sync", lambda e: e.dma_start(out=sposq[:, :], in_=sposq_d), "dT")
    vT0 = kb.val("dT")

    def rope(psrc, cos_ap, sneg_ap, spos_ap, dst_ap, deps, tag):
        N = dst_ap.shape[-1]
        for s, v in deps:
            kb.wait("vector", s, v)
        kb.wait("vector", "vS", getattr(kb, "_rope_war", 0))
        kb.op("vector", lambda e: e.tensor_tensor(t1b[:, 0:N], psrc[:, 0:N], cos_ap, ALU.mult), inc="vS")
        kb.op("vector", lambda e: e.tensor_tensor(t2b[0:64, 0:N], psrc[64:128, 0:N], sneg_ap, ALU.mult), inc="vS")
        kb.op("vector", lambda e: e.tensor_tensor(t2b[64:128, 0:N], psrc[0:64, 0:N], spos_ap, ALU.mult), inc="vS")
        v4 = kb.val("vS")
        kb.wait("vector", "vS", v4)
        r = kb.op("vector", lambda e: e.tensor_tensor(dst_ap, t1b[:, 0:N], t2b[:, 0:N], ALU.add), inc="vS")
        kb._rope_war = r
        return r

    qs_done = 0
    for h in range(NQH):
        par = h % 2
        if h >= 2:
            kb.wait("gpsimd", "tS", kb._wq_used[par])
        if not hasattr(kb, "_wq_used"):
            kb._wq_used = {}
        dv = kb.dma("gpsimd",
                    (lambda h=h, par=par: lambda e: e.dma_start(
                        out=wq[par].rearrange("p (h c) -> p h c", h=HT),
                        in_=w_q[h]))(),
                    f"dWQ{par}")
        rope_done = None
        for rb in range(NRB):
            bank = pm[rb % 2]
            kb.wait("tensor", f"dWQ{par}", dv)
            kb.wait("tensor", "ph", kb.val("ph"))
            if hasattr(kb, "_pcq_bank") and (rb % 2) in kb._pcq_bank:
                kb.wait("tensor", "vS", kb._pcq_bank[rb % 2])
            for ht in range(HT):
                mm = kb.op("tensor",
                           (lambda ht=ht, rb=rb, bank=bank, par=par: lambda e: e.matmul(
                               bank[:, 0:RB], wq[par][:, ht * 128:(ht + 1) * 128],
                               xnT_own[:, ht, rb * RB:(rb + 1) * RB],
                               start=(ht == 0), stop=(ht == HT - 1)))(),
                           inc="tS" if ht == HT - 1 else None)
            if not hasattr(kb, "_wq_used"):
                kb._wq_used = {}
            kb._wq_used[par] = kb.val("tS")
            deps = [("tS", kb.val("tS"))]
            if rb == 0 and qs_done:
                deps.append(("dQ", qs_done))
            rope_done = rope(bank,
                             cosq[:, rb * RB:(rb + 1) * RB],
                             snegq[0:64, rb * RB:(rb + 1) * RB],
                             sposq[0:64, rb * RB:(rb + 1) * RB],
                             qpiece[:, rb * RB:(rb + 1) * RB],
                             deps + [("dT", vT0)], f"q{h}")
            if not hasattr(kb, "_pcq_bank"):
                kb._pcq_bank = {}
            kb._pcq_bank[rb % 2] = kb.val("vS")
        g, hh = h // GP, h % GP
        kb.wait("sync", "vS", rope_done)
        qs_done = kb.dma("sync",
                         (lambda g=g, hh=hh: lambda e: e.dma_start(
                             out=q_d[g, :, hh, :, :].rearrange("s p q -> p s q"),
                             in_=qpiece[:, :].rearrange("p (s q) -> p s q", q=128)))(),
                         "dQ")
    kb.barrier()
    if phases <= 2:
        return _finish(nc, kb, es)

    Kc = AR[:, 0:NKVH * S]
    Vc = AR[:, 32768:32768 + NKVH * S]
    wkv = U1[:, 0:HT * 512]
    NCK = 4
    xnf = [U2[:, i * 2048:(i + 1) * 2048] for i in range(NCK)]
    ctb = [U3F[:, 1024:1536], U3F[:, 1024:1536]]
    sntb = [U3F[0:64, 1536:2048], U3F[0:64, 1536:2048]]
    sptb = [U3F[0:64, 2048:2560], U3F[0:64, 2048:2560]]
    vtmp = U3[:, 5120:5632]

    kb._pd_bankg = {u: ("tS", 0) for u in range(4)}
    kb._ptr_evac = {0: 0, 1: 0}
    for pp in range(NPASS):
        g0 = 2 * pp
        ng = min(2, NKVH - g0)
        kb.wait("gpsimd", "tS", kb.val("tS"))
        HQ = max(1, HT // 4)
        dwv = {}
        for qtr in range((HT + HQ - 1) // HQ):
            h0, h1 = qtr * HQ, min(HT, (qtr + 1) * HQ)
            dwv[qtr] = kb.dma("gpsimd",
                        (lambda pp=pp, ng=ng, h0=h0, h1=h1: lambda e: e.dma_start(
                            out=wkv.rearrange("p (h c) -> p h c", h=HT)[:, h0:h1, 0:ng * 256],
                            in_=w_kv[pp][:, h0:h1, 0:ng * 256]))(),
                        f"dWKV{qtr}")
        for tb in range(S // 512):
            tpar = tb % 2
            kb.wait("scalar", "vS", kb._pd_trig_g if hasattr(kb, "_pd_trig_g") else 0)
            for dst, src in ((ctb[tpar], cosd_d[:, tb * 512:(tb + 1) * 512]),
                             (sntb[tpar], sneg_d[:, tb * 512:(tb + 1) * 512]),
                             (sptb[tpar], spos_d[:, tb * 512:(tb + 1) * 512])):
                kb.dma("scalar", (lambda d=dst, s=src: lambda e: e.dma_start(out=d, in_=s))(), "dTT")
            vtrig = kb.val("dTT")
            nck = (HT + 3) // 4
            for ck in range(nck):
                cpar = (tb * nck + ck) % NCK
                nht = min(4, HT - ck * 4)
                if hasattr(kb, "_xnf_used") and cpar in kb._xnf_used:
                    kb.wait("sync", "tS", kb._xnf_used[cpar])
                dv = kb.dma("sync",
                            (lambda ck=ck, cpar=cpar, tb=tb, nht=nht: lambda e: e.dma_start(
                                out=xnf[cpar].rearrange("p (a t) -> p a t", a=4)[:, 0:nht, :],
                                in_=xnT_full_d.rearrange("h p t -> p h t")[:, ck * 4:ck * 4 + nht,
                                                                           tb * 512:(tb + 1) * 512]))(),
                            f"dXN{cpar}")
                kb.wait("tensor", f"dXN{cpar}", dv)
                if ck == 0:
                    for u, (semn, valn) in kb._pd_bankg.items():
                        kb.wait("tensor", semn, valn)
                for hl in range(nht):
                    ht = ck * 4 + hl
                    kb.wait("tensor", f"dWKV{ht // HQ}", dwv[ht // HQ])
                    for u in range(2 * ng):
                        mm = kb.op("tensor",
                                   (lambda u=u, ht=ht, hl=hl, cpar=cpar: lambda e: e.matmul(
                                       pm[u][:, :], wkv[:, ht * 512 + u * 128: ht * 512 + (u + 1) * 128],
                                       xnf[cpar][:, hl * 512:(hl + 1) * 512],
                                       start=(ht == 0), stop=(ht == HT - 1)))(),
                                   inc="tS" if (hl == nht - 1 and u == 2 * ng - 1) else None)
                if not hasattr(kb, "_xnf_used"):
                    kb._xnf_used = {}
                kb._xnf_used[cpar] = kb.val("tS") + (0 if ck < nck - 1 else 0)
            tmm = kb.val("tS")
            for gl in range(ng):
                g = g0 + gl
                rope_done = rope(pm[2 * gl],
                                 ctb[tpar][:, :], sntb[tpar][0:64, :], sptb[tpar][0:64, :],
                                 Kc[:, g * S + tb * 512: g * S + (tb + 1) * 512],
                                 [("tS", tmm), ("dTT", vtrig)], f"k{g}t{tb}")
                kb._pd_bankg[2 * gl] = ("vS", rope_done)
                kb._pd_trig_g = rope_done
                kb.wait("scalar", "tS", tmm)
                if hasattr(kb, "_vtmp_used"):
                    kb.wait("scalar", "tS", kb._vtmp_used)
                sc = kb.op("scalar",
                           (lambda gl=gl: lambda e: e.copy(vtmp[:, :], pm[2 * gl + 1][:, :]))(),
                           inc="sS")
                kb._pd_bankg[2 * gl + 1] = ("sS", sc)
                slot = tb % 2
                kb.wait("tensor", "sS", sc)
                kb.wait("tensor", "vT", kb._ptr_evac[slot])
                for i in range(4):
                    kb.op("tensor",
                          (lambda i=i, slot=slot: lambda e: e.transpose(
                              ptr[:, slot * 1024 + i * 128: slot * 1024 + (i + 1) * 128],
                              vtmp[:, i * 128:(i + 1) * 128], identb[:, :]))(),
                          inc="tS" if i == 3 else None)
                kb._vtmp_used = kb.val("tS")
                kb.wait("vector", "tS", kb.val("tS"))
                ev = kb.op("vector",
                           (lambda g=g, tb=tb, slot=slot: lambda e: e.tensor_copy(
                               Vc[:, g * S + tb * 512: g * S + (tb + 1) * 512],
                               ptr[:, slot * 1024: slot * 1024 + 512]))(),
                           inc="vT")
                kb._ptr_evac[slot] = ev
    kb.barrier()
    if phases <= 3:
        return _finish(nc, kb, es)

    QGW = SL * GP * 128
    qg = [U2[:, i * QGW:(i + 1) * QGW] for i in range(2)]
    maskbuf = U1[:, 0:SL * 4 * 512]
    pbuf = [U3[:, i * 1024:(i + 1) * 1024] for i in range(4)]
    stage = [U3[:, 4096 + i * 512:4096 + (i + 1) * 512] for i in range(2)]
    onesb = U3[:, 5120:5136]
    dm = kb.dma("sync", lambda e: e.dma_start(
        out=maskbuf[:, :].rearrange("p (m q) -> p m q", q=512),
        in_=masks_d.rearrange("m p q -> p m q")), "dM")
    kb.dma("sync", lambda e: e.dma_start(out=onesb[:, 0:1], in_=onesb_d), "dM")
    dm = kb.val("dM")

    kb._pe_pv_guard = {0: 0, 1: 0}
    kb._pe_pl_guard = {0: 0, 1: 0}
    kb._pe_pbc_guard = 0
    kb._pe_sc_guard = {0: 0, 1: 0}
    kb._pe_p_guard = {i: (0, 0) for i in range(4)}
    kb._pe_stage_guard = {0: 0, 1: 0}
    qload = {}

    def emit_scores(lane, g, s, par, bb, Tn):
        pb_ = 2 * lane + (bb % 2)
        tts = (2 * bb, 2 * bb + 1)
        kb.wait("tensor", qload[g][0], qload[g][1])
        kb.wait("tensor", "sS", kb._pe_sc_guard[lane])
        if bb == 0:
            kb.wait("tensor", "dM", dm)
            kb.wait("tensor", "vS", kb._pe_pv_guard[lane])
            kb.wait("tensor", "vS", kb._pe_pl_guard[lane])
        for k, tt in enumerate(tts):
            smm = kb.op("tensor",
                        (lambda g=g, s=s, tt=tt, lane=lane, k=k, par=par: lambda e: e.matmul(
                            PM[:, lane * 1024 + k * 512: lane * 1024 + (k + 1) * 512],
                            Kc[:, g * S + tt * 128:g * S + (tt + 1) * 128],
                            qg[par][:, s * GP * 128:(s + 1) * GP * 128],
                            start=True, stop=True))(),
                        inc="tS" if k == 1 else None)
        kb.wait("scalar", "tS", smm)
        kb.wait("scalar", "tS", kb._pe_p_guard[pb_][0])
        kb.wait("scalar", "vS", kb._pe_p_guard[pb_][1])
        ex = kb.op("scalar",
                   (lambda lane=lane, pb_=pb_: lambda e: e.activation(
                       pbuf[pb_][:, :], PM[:, lane * 1024:(lane + 1) * 1024],
                       AF.Exp, bias=EB, scale=SC))(),
                   inc="sS")
        kb._pe_sc_guard[lane] = ex
        mg = 0
        for k, tt in enumerate(tts):
            dt = tt - (Tn - 4)
            if dt >= 0:
                kb.wait("vector", "sS", ex)
                kb.wait("vector", "dM", dm)
                kb.op("vector",
                      (lambda s=s, dt=dt, pb_=pb_, k=k: lambda e: e.tensor_tensor(
                          pbuf[pb_][:, k * 512:(k + 1) * 512],
                          pbuf[pb_][:, k * 512:(k + 1) * 512],
                          maskbuf[:, (s * 4 + dt) * 512:(s * 4 + dt + 1) * 512],
                          ALU.mult))(),
                      inc="vS")
                mg = kb.val("vS")
        return (tts, pb_, mg, ex)

    def emit_pvl(lane, g, Tn, pend):
        tts, pb_, mg, ex = pend
        if mg:
            kb.wait("tensor", "vS", mg)
        else:
            kb.wait("tensor", "sS", ex)
        lm = 0
        for k, tt in enumerate(tts):
            kb.op("tensor",
                  (lambda g=g, tt=tt, pb_=pb_, k=k, Tn=Tn, lane=lane: lambda e: e.matmul(
                      ptrF[lane][:, :], Vc[:, g * S + tt * 128:g * S + (tt + 1) * 128],
                      pbuf[pb_][:, k * 512:(k + 1) * 512],
                      start=(tt == 0), stop=(tt == Tn - 1)))(),
                  inc=None)
            lm = kb.op("tensor",
                       (lambda tt=tt, pb_=pb_, k=k, Tn=Tn, lane=lane: lambda e: e.matmul(
                           pl[64 * lane:64 * lane + 1, :], onesb[:, 0:1],
                           pbuf[pb_][:, k * 512:(k + 1) * 512],
                           start=(tt == 0), stop=(tt == Tn - 1)))(),
                       inc="tS")
        kb._pe_p_guard[pb_] = (lm, mg)

    def emit_tail(lane, g, s):
        pv_done = kb.val("tS")
        kb.wait("vector", "tS", pv_done)
        rc = kb.op("vector",
                   (lambda lane=lane: lambda e: e.reciprocal(
                       recip[0:1, lane * 512:(lane + 1) * 512],
                       pl[64 * lane:64 * lane + 1, :]))(), inc="vS")
        kb._pe_pl_guard[lane] = rc
        kb.wait("tensor", "vS", rc)
        kb.wait("tensor", "sS", kb._pe_pbc_guard)
        bc = kb.op("tensor",
                   (lambda lane=lane: lambda e: e.matmul(
                       pbc[:, :], onesr[:, :], recip[0:1, lane * 512:(lane + 1) * 512],
                       start=True, stop=True))(), inc="tS")
        kb.wait("scalar", "tS", bc)
        cp = kb.op("scalar", lambda e: e.copy(bcs[:, :], pbc[:, :]), inc="sS")
        kb._pe_pbc_guard = cp
        spar = (g * SL + s) % 2
        kb.wait("vector", "sS", cp)
        kb.wait("vector", f"dA{spar}", kb._pe_stage_guard[spar])
        dv = kb.op("vector",
                   (lambda spar=spar, lane=lane: lambda e: e.tensor_tensor(
                       stage[spar][:, :], ptrF[lane][:, :], bcs[:, :], ALU.mult))(),
                   inc="vS")
        kb._pe_pv_guard[lane] = dv
        kb.wait("sync", "vS", dv)
        stv = kb.dma("sync",
                     (lambda g=g, s=s, spar=spar: lambda e: e.dma_start(
                         out=attnT_d.rearrange("h p r -> p h r")[:, g * GP:(g + 1) * GP,
                                                                 s * 128:(s + 1) * 128],
                         in_=stage[spar][:, :].rearrange("p (h q) -> p h q", h=GP)))(),
                     f"dA{spar}")
        kb._pe_stage_guard[spar] = stv

    for g in range(NKVH):
        par = g % 2
        if g not in qload:
            if g >= 2:
                kb.wait("sync", "tS", kb._qg_used[par])
            qload[g] = (f"dQG{par}", kb.dma("sync",
                              (lambda g=g, par=par: lambda e: e.dma_start(
                                  out=qg[par][:, :].rearrange("p (s h q) -> p s h q", s=SL, h=GP),
                                  in_=q_d[g].rearrange("s h p q -> p s h q")))(),
                              f"dQG{par}"))
        if g + 1 < NKVH:
            par2 = (g + 1) % 2
            if g + 1 >= 2:
                kb.wait("sync", "tS", kb._qg_used.get(par2, 0) if hasattr(kb, "_qg_used") else 0)
            qload[g + 1] = (f"dQG{par2}", kb.dma("sync",
                                  (lambda g=g + 1, par=par2: lambda e: e.dma_start(
                                      out=qg[par][:, :].rearrange("p (s h q) -> p s h q", s=SL, h=GP),
                                      in_=q_d[g].rearrange("s h p q -> p s h q")))(),
                                  f"dQG{par2}"))
        for j in range(SL // 2):
            sA, sB = 2 * j, 2 * j + 1
            TA, TB = c["T"][sA], c["T"][sB]
            pend = {0: None, 1: None}
            for bb in range(TA // 2):
                pend_new_a = emit_scores(0, g, sA, par, bb, TA)
                if pend[1] is not None:
                    emit_pvl(1, g, TB, pend[1])
                    pend[1] = None
                if bb < TB // 2:
                    pend[1] = emit_scores(1, g, sB, par, bb, TB)
                if pend[0] is not None:
                    emit_pvl(0, g, TA, pend[0])
                pend[0] = pend_new_a
            if pend[1] is not None:
                emit_pvl(1, g, TB, pend[1])
            if pend[0] is not None:
                emit_pvl(0, g, TA, pend[0])
            emit_tail(0, g, sA)
            emit_tail(1, g, sB)
        if not hasattr(kb, "_qg_used"):
            kb._qg_used = {}
        kb._qg_used[par] = kb.val("tS")
    kb.barrier()
    if phases <= 4:
        return _finish(nc, kb, es)

    attnT = AR[:, 0:HT * OWN].rearrange("p (h r) -> p h r", h=HT)
    wo = [U1[:, i * (HT * 128):(i + 1) * (HT * 128)] for i in range(2)]
    ybuf = [U2F[:, i * 512:(i + 1) * 512] for i in range(2)]
    kb.dma("sync", lambda e: e.dma_start(
        out=attnT[:, :, :], in_=attnT_d.rearrange("h p r -> p h r")), "dAT")
    vat = kb.val("dAT")
    kb._pf_w_used = {}
    kb._pf_bank = {}
    kb._pf_ybuf = {0: 0, 1: 0}
    for of in range(HT):
        par = of % 2
        if of >= 2:
            kb.wait("gpsimd", "tS", kb._pf_w_used[par])
        dw = kb.dma("gpsimd",
                    (lambda of=of, par=par: lambda e: e.dma_start(
                        out=wo[par].rearrange("p (h c) -> p h c", h=HT),
                        in_=w_o[of]))(),
                    f"dWO{par}")
        for rb in range(NRB):
            bank = pm[(of * NRB + rb) % 4]
            kb.wait("tensor", f"dWO{par}", dw)
            kb.wait("tensor", "dAT", vat)
            bg = kb._pf_bank.get((of * NRB + rb) % 4)
            if bg:
                kb.wait("tensor", "vS", bg)
            for ht in range(HT):
                kb.op("tensor",
                      (lambda ht=ht, rb=rb, bank=bank, par=par: lambda e: e.matmul(
                          bank[:, 0:RB], wo[par][:, ht * 128:(ht + 1) * 128],
                          attnT[:, ht, rb * RB:(rb + 1) * RB],
                          start=(ht == 0), stop=(ht == HT - 1)))(),
                      inc="tS" if ht == HT - 1 else None)
            kb._pf_w_used[par] = kb.val("tS")
            ypar = (of * NRB + rb) % 2
            kb.wait("vector", "tS", kb.val("tS"))
            kb.wait("vector", f"dY{ypar}", kb._pf_ybuf[ypar])
            ev = kb.op("vector",
                       (lambda bank=bank, ypar=ypar: lambda e: e.tensor_copy(
                           ybuf[ypar][:, 0:RB], bank[:, 0:RB]))(),
                       inc="vS")
            kb._pf_bank[(of * NRB + rb) % 4] = ev
            kb.wait("sync", "vS", ev)
            sy = kb.dma("sync",
                        (lambda of=of, rb=rb, ypar=ypar: lambda e: e.dma_start(
                            out=y_d[of, :, rb * RB:(rb + 1) * RB],
                            in_=ybuf[ypar][:, 0:RB]))(),
                        f"dY{ypar}")
            kb._pf_ybuf[ypar] = sy
    kb.barrier()
    if phases <= 5:
        return _finish(nc, kb, es)

    xnA = AR[:, 0:HT * OWN].rearrange("p (h r) -> p h r", h=HT)
    nsb2 = min(8, IT)
    base2 = IT // nsb2
    SBS2 = [base2 + (1 if i < IT % nsb2 else 0) for i in range(nsb2)]
    SBmax = max(SBS2)
    inter2 = [AR[:, HT * OWN + i * SBmax * OWN: HT * OWN + (i + 1) * SBmax * OWN]
              .rearrange("p (i r) -> p i r", r=OWN) for i in range(2)]
    wgu = [U1[:, i * (2 * HT * 128):(i + 1) * (2 * HT * 128)] for i in range(2)]
    NWD = 2
    wd = [U2[:, i * (2 * SBmax * 128):(i + 1) * (2 * SBmax * 128)] for i in range(NWD)]
    sg = [U2F[:, 3072 + i * 512:3072 + (i + 1) * 512] for i in range(2)]
    fbuf = [U3F[:, i * OWN:(i + 1) * OWN] for i in range(2)]
    kb.dma("sync", lambda e: e.dma_start(
        out=xnA[:, :, :], in_=xnT_own_d.rearrange("h p r -> p h r")), "dXA")
    vxa = kb.val("dXA")
    kb._pg_w_used = {0: 0, 1: 0}
    kb._pg_sg = {0: 0, 1: 0}
    kb._pg_fbuf = {0: 0, 1: 0}
    kb._pg_islot = {0: 0, 1: 0}
    kb._pg_fbank = {0: 0, 1: 0}
    kb._pg_wd_used = {0: 0, 1: 0, 2: 0}
    sb_starts = []
    t = 0
    for n_ in SBS2:
        sb_starts.append(t)
        t += n_

    def emit_gateup(sbi):
        it0 = sb_starts[sbi]
        sbn = SBS2[sbi]
        islot = sbi % 2
        inter_slot = inter2[islot]
        kb.wait("tensor", "tS", kb._pg_islot[islot])
        for itl in range(sbn):
            it = it0 + itl
            par = it % 2
            kb.wait("gpsimd", "tS", kb._pg_w_used[par])
            dgv = kb.dma("gpsimd",
                         (lambda it=it, par=par: lambda e: e.dma_start(
                             out=wgu[par][:, :], in_=w_guT[it]))(),
                         f"dWG{par}")
            for rb in range(NRB):
                bg_ = pm[rb % 2]
                bu_ = pm[2 + rb % 2]
                kb.wait("tensor", f"dWG{par}", dgv)
                kb.wait("tensor", "dXA", vxa)
                kb.wait("tensor", "sS", kb._pg_gate_guard.get(rb % 2, 0) if hasattr(kb, "_pg_gate_guard") else 0)
                kb.wait("tensor", "vS", kb._pg_up_guard.get(rb % 2, 0) if hasattr(kb, "_pg_up_guard") else 0)
                for ht in range(HT):
                    kb.op("tensor",
                          (lambda ht=ht, rb=rb, bg_=bg_, par=par: lambda e: e.matmul(
                              bg_[:, 0:RB], wgu[par][:, ht * 128:(ht + 1) * 128],
                              xnA[:, ht, rb * RB:(rb + 1) * RB],
                              start=(ht == 0), stop=(ht == HT - 1)))(),
                          inc="tS" if ht == HT - 1 else None)
                gmm = kb.val("tS")
                for ht in range(HT):
                    kb.op("tensor",
                          (lambda ht=ht, rb=rb, bu_=bu_, par=par: lambda e: e.matmul(
                              bu_[:, 0:RB], wgu[par][:, HT * 128 + ht * 128:HT * 128 + (ht + 1) * 128],
                              xnA[:, ht, rb * RB:(rb + 1) * RB],
                              start=(ht == 0), stop=(ht == HT - 1)))(),
                          inc="tS" if ht == HT - 1 else None)
                umm = kb.val("tS")
                kb._pg_w_used[par] = umm
                spar = rb % 2
                kb.wait("scalar", "tS", gmm)
                kb.wait("scalar", "vS", kb._pg_sg[spar])
                sl_ = kb.op("scalar",
                            (lambda bg_=bg_, spar=spar: lambda e: e.activation(
                                sg[spar][:, 0:RB], bg_[:, 0:RB], AF.Silu))(),
                            inc="sS")
                if not hasattr(kb, "_pg_gate_guard"):
                    kb._pg_gate_guard = {}
                    kb._pg_up_guard = {}
                kb._pg_gate_guard[spar] = sl_
                kb.wait("vector", "tS", umm)
                kb.wait("vector", "sS", sl_)
                iv = kb.op("vector",
                           (lambda itl=itl, rb=rb, bu_=bu_, spar=spar, inter_slot=inter_slot: lambda e: e.tensor_tensor(
                               inter_slot[:, itl, rb * RB:(rb + 1) * RB], bu_[:, 0:RB],
                               sg[spar][:, 0:RB], ALU.mult))(),
                           inc="vS")
                kb._pg_up_guard[spar] = iv
                kb._pg_sg[spar] = iv
        return kb.val("vS")

    def emit_down(sbi, inter_done):
        it0 = sb_starts[sbi]
        sbn = SBS2[sbi]
        islot = sbi % 2
        inter_slot = inter2[islot]
        pend = []
        for ht0 in range(0, HT, 2):
            nh = min(2, HT - ht0)
            par = (ht0 // 2) % NWD
            kb.wait("gpsimd", "tS", kb._pg_wd_used[par])
            ddv = kb.dma("gpsimd",
                         (lambda it0=it0, sbn=sbn, ht0=ht0, nh=nh, par=par: lambda e: e.dma_start(
                             out=wd[par].rearrange("p (u i c) -> p u i c", u=2, c=128)[:, 0:nh, 0:sbn, :],
                             in_=w_dT[ht0:ht0 + nh].rearrange("u p i c -> p u i c")[:, :, it0:it0 + sbn, :]))(),
                         f"dWD{par}")
            for pfpar, pht, pev in pend:
                kb.wait("gpsimd", "vS", pev)
                kb._pg_fbuf[pfpar] = kb.dma("gpsimd",
                            (lambda ht=pht, fpar=pfpar: lambda e: e.dma_start(
                                out=y_d[ht, :, :],
                                in_=fbuf[fpar][:, 0:OWN], accum_op=ALU.add))(),
                            f"dFA{pfpar}")
            pend = []
            for u in range(nh):
                ht = ht0 + u
                fpar = ht % 2
                kb.wait("vector", f"dFA{fpar}", kb._pg_fbuf[fpar])
                ev = 0
                for rb in range(NRB):
                    bk = (ht * NRB + rb) % 2
                    bank = ptrF[bk]
                    kb.wait("tensor", f"dWD{par}", ddv)
                    kb.wait("tensor", "vS", inter_done)
                    kb.wait("tensor", "vS", kb._pg_fbank[bk])
                    for itl in range(sbn):
                        kb.op("tensor",
                              (lambda itl=itl, rb=rb, bank=bank, u=u, par=par, inter_slot=inter_slot: lambda e: e.matmul(
                                  bank[:, 0:RB], wd[par][:, (u * SBmax + itl) * 128:(u * SBmax + itl + 1) * 128],
                                  inter_slot[:, itl, rb * RB:(rb + 1) * RB],
                                  start=(itl == 0), stop=(itl == sbn - 1)))(),
                              inc="tS" if itl == sbn - 1 else None)
                    kb._pg_wd_used[par] = kb.val("tS")
                    kb.wait("vector", "tS", kb.val("tS"))
                    ev = kb.op("vector",
                               (lambda bank=bank, fpar=fpar, rb=rb: lambda e: e.tensor_copy(
                                   fbuf[fpar][:, rb * RB:(rb + 1) * RB], bank[:, 0:RB]))(),
                               inc="vS")
                    kb._pg_fbank[bk] = ev
                pend.append((fpar, ht, ev))
        for pfpar, pht, pev in pend:
            kb.wait("gpsimd", "vS", pev)
            kb._pg_fbuf[pfpar] = kb.dma("gpsimd",
                        (lambda ht=pht, fpar=pfpar: lambda e: e.dma_start(
                            out=y_d[ht, :, :],
                            in_=fbuf[fpar][:, 0:OWN], accum_op=ALU.add))(),
                        f"dFA{pfpar}")
        kb._pg_islot[islot] = kb.val("tS")

    pend_sb = None
    for sbi in range(nsb2):
        idn = emit_gateup(sbi)
        if pend_sb is not None:
            emit_down(pend_sb[0], pend_sb[1])
        pend_sb = (sbi, idn)
    emit_down(pend_sb[0], pend_sb[1])
    kb.barrier()
    if phases <= 6:
        return _finish(nc, kb, es)

    NPH = 3
    yv = [ARF[:, i * HT * 128:(i + 1) * HT * 128] for i in range(NPH)]
    outv = [ARF[:, (NPH + i) * HT * 128:(NPH + 1 + i) * HT * 128] for i in range(NPH)]
    xot = [U1F[:, 0:D], U1F[:, D:2 * D], ARF[:, 2 * NPH * HT * 128:2 * NPH * HT * 128 + D]]
    kb._ph_out_guard = {i: 0 for i in range(NPH)}
    kb._ph_yv = {i: 0 for i in range(NPH)}
    kb._ph_xot = {i: 0 for i in range(NPH)}
    for ot in range(OWN // 128):
        par = ot % NPH
        kb.wait("sync", "tS", kb._ph_yv[par])
        dy = kb.dma("sync",
                    (lambda ot=ot, par=par: lambda e: e.dma_start(
                        out=yv[par].rearrange("p (h r) -> p h r", h=HT),
                        in_=y_d.rearrange("h p r -> p h r")[:, :, ot * 128:(ot + 1) * 128]))(),
                    f"dYL{par}")
        kb.wait("sync", "vS", kb._ph_xot[par])
        dx = kb.dma("sync",
                    (lambda ot=ot, par=par: lambda e: e.dma_start(
                        out=xot[par][:, :], in_=x_own[ot * 128:(ot + 1) * 128, :]))(),
                    f"dXL{par}")
        for ht0 in range(0, HT, 4):
            bank = pm[(ht0 // 4) % 2]
            kb.wait("tensor", f"dYL{par}", dy)
            bg = kb._ph_bank.get((ht0 // 4) % 2) if hasattr(kb, "_ph_bank") else None
            if bg:
                kb.wait("tensor", "vS", bg)
            for k in range(4):
                ht = ht0 + k
                kb.op("tensor",
                      (lambda ht=ht, k=k, bank=bank, par=par: lambda e: e.transpose(
                          bank[:, k * 128:(k + 1) * 128], yv[par][:, ht * 128:(ht + 1) * 128],
                          identf[:, :]))(),
                      inc="tS" if k == 3 else None)
            tv = kb.val("tS")
            kb.wait("vector", "tS", tv)
            kb.wait("vector", f"dXL{par}", dx)
            kb.wait("vector", f"dOUT{par}", kb._ph_out_guard[par])
            av = kb.op("vector",
                       (lambda ht0=ht0, bank=bank, par=par: lambda e: e.tensor_tensor(
                           outv[par][:, ht0 * 128:(ht0 + 4) * 128], bank[:, 0:512],
                           xot[par][:, ht0 * 128:(ht0 + 4) * 128], ALU.add))(),
                       inc="vS")
            if not hasattr(kb, "_ph_bank"):
                kb._ph_bank = {}
            kb._ph_bank[(ht0 // 4) % 2] = av
        kb._ph_yv[par] = kb.val("tS")
        kb._ph_xot[par] = kb.val("vS")
        kb.wait("sync", "vS", av)
        kb._ph_out_guard[par] = kb.dma("sync",
               (lambda ot=ot, par=par: lambda e: e.dma_start(
                   out=out_d[ot * 128:(ot + 1) * 128, :], in_=outv[par][:, :]))(),
               f"dOUT{par}")
    kb.wait("sync", "dOUT0", kb.val("dOUT0"))
    kb.wait("sync", "dOUT1", kb.val("dOUT1"))
    kb.barrier()

    return _finish(nc, kb, es)


def _finish(nc, kb, es):
    semobjs = {}
    for name in kb.sem_names:
        semobjs[name] = es.enter_context(nc.semaphore(name))
    with es:
        with nc.Block() as block:
            block.sync(kb.runner("sync", semobjs))
            block.gpsimd(kb.runner("gpsimd", semobjs))
            block.tensor(kb.runner("tensor", semobjs))
            block.vector(kb.runner("vector", semobjs))
            block.scalar(kb.runner("scalar", semobjs))
    return nc


def rope_tables(cfg, positions):
    HD = 128
    invf = cfg["theta"] ** (-np.arange(0, HD, 2, dtype=np.float64) / HD)
    ang = np.asarray(positions, np.float64)[None, :] * invf[:, None]
    cos = np.cos(ang).astype(np.float32)
    sin = np.sin(ang).astype(np.float32)
    cosd = np.concatenate([cos, cos], 0)
    return cosd, (-sin).astype(np.float32), sin.astype(np.float32)


def perm_even_odd():
    return np.concatenate([np.arange(0, 128, 2), np.arange(1, 128, 2)])


def make_host_inputs(inputs, cfg):
    c = derive(cfg)
    D, S, SL, OWN = c["D"], c["S"], c["SL"], c["OWN"]
    NQH, NKVH = c["NQH"], c["NKVH"]
    HD = 128
    x = np.asarray(inputs["L0_attn_x"], np.float32)
    w_qkv = np.asarray(inputs["L0_attn_w_qkv"], np.float32)
    w_o = np.ascontiguousarray(np.asarray(inputs["L0_attn_w_o"], np.float32))
    w_g = np.asarray(inputs["L0_ffn_w_gate"], np.float32)
    w_u = np.asarray(inputs["L0_ffn_w_up"], np.float32)
    w_dn = np.asarray(inputs["L0_ffn_w_down"], np.float32)

    HT = c["HT"]; IT = c["IT"]
    NPASS = (NKVH + 1) // 2
    PW = min(2, NKVH) * 256
    perm = perm_even_odd()
    wq = w_qkv[:, :D].reshape(D, NQH, HD)[:, :, perm].reshape(D, D)
    wk = w_qkv[:, D:D + NKVH * HD].reshape(D, NKVH, HD)[:, :, perm]
    wv = w_qkv[:, D + NKVH * HD:].reshape(D, NKVH, HD)
    w_kv = np.concatenate([wk, wv], axis=2).reshape(D, NKVH * 256)
    w_q_t = np.ascontiguousarray(
        wq.reshape(HT, 128, NQH, 128).transpose(2, 1, 0, 3))
    w_kv_t = np.ascontiguousarray(
        w_kv.reshape(HT, 128, NPASS, PW).transpose(2, 1, 0, 3))
    w_o_t = np.ascontiguousarray(
        w_o.reshape(HT, 128, HT, 128).transpose(2, 1, 0, 3))
    w_g_t = w_g.reshape(IT, 128, HT, 128).transpose(0, 3, 2, 1)
    w_u_t = w_u.reshape(IT, 128, HT, 128).transpose(0, 3, 2, 1)
    w_gu_t = np.ascontiguousarray(
        np.concatenate([w_g_t.reshape(IT, 128, HT * 128),
                        w_u_t.reshape(IT, 128, HT * 128)], axis=2))
    w_d_t = np.ascontiguousarray(
        w_dn.reshape(HT, 128, IT, 128).transpose(0, 3, 2, 1))

    cosd, sneg, spos = rope_tables(cfg, np.arange(S))
    identb = np.eye(128, dtype=bfdt)
    identf = np.eye(128, dtype=np.float32)
    onesc = np.ones((128, 1), np.float32)
    onesr = np.ones((1, 128), np.float32)

    shared = dict(w_q=w_q_t, w_kv=w_kv_t,
                  w_o=w_o_t, w_guT=w_gu_t, w_dT=w_d_t,
                  identb=identb, identf=identf, onesc=onesc, onesr=onesr,
                  onesb=np.ones((128, 1), bfdt),
                  cosd=np.ascontiguousarray(cosd), sneg=np.ascontiguousarray(sneg),
                  spos=np.ascontiguousarray(spos))

    in_maps = []
    metas = []
    for core in range(8):
        b, rank = core // RANKS, core % RANKS
        pos = [snake_pos(c, rank, s) for s in range(SL)]
        rows = np.concatenate([np.arange(p * 128, (p + 1) * 128) for p in pos])
        x_own = np.ascontiguousarray(x[b][rows])
        cosq, snegq, sposq = rope_tables(cfg, rows)
        masks = np.zeros((SL * 4, 128, 512), dtype=bfdt)
        tloc = np.arange(128)[:, None]
        qloc = np.arange(128)[None, :]
        for s in range(SL):
            for dt in range(4):
                tt = c["T"][s] - 4 + dt
                m = ((tt * 128 + tloc) <= (pos[s] * 128 + qloc)).astype(np.float32)
                masks[s * 4 + dt] = np.tile(m, (1, 4)).astype(bfdt)
        im = dict(shared)
        im.update(x_own=x_own, x_full=np.ascontiguousarray(x[b]),
                  cosq=np.ascontiguousarray(cosq),
                  snegq=np.ascontiguousarray(snegq),
                  sposq=np.ascontiguousarray(sposq), masks=masks)
        in_maps.append(im)
        metas.append((b, rows))
    return in_maps, metas


def assemble_output(cfg, results, metas):
    c = derive(cfg)
    out = np.empty((cfg["B"], c["S"], c["D"]), np.float32)
    for core, ((b, rows), r) in enumerate(zip(metas, results)):
        out[b][rows] = np.asarray(r["out_own"], np.float32)
    return out


def kernel(L0_attn_x, L0_attn_w_qkv, L0_attn_w_o, L0_ffn_w_gate, L0_ffn_w_up,
           L0_ffn_w_down):
    cfg = full_cfg()
    inputs = dict(L0_attn_x=L0_attn_x, L0_attn_w_qkv=L0_attn_w_qkv,
                  L0_attn_w_o=L0_attn_w_o, L0_ffn_w_gate=L0_ffn_w_gate,
                  L0_ffn_w_up=L0_ffn_w_up, L0_ffn_w_down=L0_ffn_w_down)
    in_maps, metas = make_host_inputs(inputs, cfg)
    nc = build_program(cfg)
    res = run_bass_kernel_spmd(nc, in_maps, core_ids=list(range(8)))
    return assemble_output(cfg, res.results, metas)


# version 14 (final)
# speedup vs baseline: 1.1161x; 1.1161x over previous
import sys
if "/opt/trn_rl_repo" not in sys.path:
    sys.path.insert(0, "/opt/trn_rl_repo")
import math
import contextlib
import numpy as np
import ml_dtypes
import concourse.bass as bass
import concourse.mybir as mybir
from concourse.bass_utils import run_bass_kernel_spmd

F32 = mybir.dt.float32
BF16 = mybir.dt.bfloat16
AF = mybir.ActivationFunctionType
ALU = mybir.AluOpType
bfdt = ml_dtypes.bfloat16
f32 = np.float32

ENGINES = ("sync", "gpsimd", "tensor", "vector", "scalar")
RANKS = 4


def full_cfg():
    return dict(B=2, S=4096, D=4096, HD=128, NQH=32, NKVH=8, I=11008,
                theta=10000.0, eps=1e-5)


def derive(cfg):
    c = dict(cfg)
    c["HT"] = c["D"] // 128
    c["ST"] = c["S"] // 128
    c["SL"] = c["ST"] // RANKS
    c["OWN"] = c["SL"] * 128
    c["RB"] = min(512, c["OWN"])
    c["NRB"] = c["OWN"] // c["RB"]
    c["IT"] = c["I"] // 128
    nsb = 4
    base = c["IT"] // nsb
    rem = c["IT"] % nsb
    c["SBS"] = [base + (1 if i < rem else 0) for i in range(nsb)]
    c["T"] = [c["ST"] - RANKS * s for s in range(c["SL"])]
    c["GP"] = 4
    assert c["NQH"] == c["NKVH"] * c["GP"]
    assert c["HD"] == 128
    return c


def snake_pos(cfg, rank, s):
    base = cfg["ST"] - 1 - RANKS * s
    return base - rank if s % 2 == 0 else base - (RANKS - 1) + rank


class KB:
    def __init__(self):
        self.q = {e: [] for e in ENGINES}
        self.cnt = {}
        self.waited = {e: {} for e in ENGINES}
        self.last_real = {e: None for e in ENGINES}
        self.dma_sems = {e: set() for e in ENGINES}
        self.dma_since_bar = {e: False for e in ENGINES}
        self.sem_names = []

    def sem(self, name):
        if name not in self.cnt:
            self.cnt[name] = 0
            self.sem_names.append(name)
        return name

    def val(self, sem):
        return self.cnt.get(sem, 0)

    def op(self, eng, fn, inc=None):
        rec = {"fn": fn, "incs": []}
        if inc is not None:
            self.sem(inc)
            self.cnt[inc] += 1
            rec["incs"].append((inc, 1))
            rec["val"] = (inc, self.cnt[inc])
        self.q[eng].append(rec)
        self.last_real[eng] = rec
        return self.cnt[inc] if inc is not None else None

    def dma(self, eng, fn, sem):
        self.sem(sem)
        self.cnt[sem] += 16
        rec = {"fn": fn, "incs": [(sem, 16)]}
        self.q[eng].append(rec)
        self.dma_sems[eng].add(sem)
        self.dma_since_bar[eng] = True
        return self.cnt[sem]

    def wait(self, eng, sem, val):
        if sem is None or val is None or val <= 0:
            return
        w = self.waited[eng]
        if w.get(sem, 0) >= val:
            return
        w[sem] = val
        self.q[eng].append({"wait": (sem, val)})

    def barrier(self):
        ph = self.sem("ph")
        for e in ENGINES:
            if self.dma_since_bar[e]:
                for dsem in sorted(self.dma_sems[e]):
                    self.wait(e, dsem, self.cnt[dsem])
                self.cnt[ph] += 1
                self.q[e].append({"nopinc": (ph, 1)})
                self.dma_since_bar[e] = False
            if self.last_real[e] is not None:
                rec = self.last_real[e]
                if rec["incs"]:
                    s, v = rec["val"]
                    self.wait(e, s, v)
                    self.cnt[ph] += 1
                    self.q[e].append({"nopinc": (ph, 1)})
                else:
                    self.cnt[ph] += 1
                    rec["incs"].append((ph, 1))
                    rec["val"] = (ph, self.cnt[ph])
                self.last_real[e] = None
        tgt = self.cnt[ph]
        for e in ENGINES:
            self.wait(e, ph, tgt)

    def runner(self, eng, semobjs):
        lst = self.q[eng]

        def run(e):
            for rec in lst:
                if "wait" in rec:
                    s, v = rec["wait"]
                    e.wait_ge(semobjs[s], v)
                elif "nopinc" in rec:
                    s, n = rec["nopinc"]
                    e.nop().then_inc(semobjs[s], n)
                else:
                    ins = rec["fn"](e)
                    for s, n in rec["incs"]:
                        ins = ins.then_inc(semobjs[s], n)
        return run


def build_program(cfg, phases=99):
    c = derive(cfg)
    D, S, HT, ST, SL = c["D"], c["S"], c["HT"], c["ST"], c["SL"]
    OWN, RB, NRB = c["OWN"], c["RB"], c["NRB"]
    NQH, NKVH, GP, IT = c["NQH"], c["NKVH"], c["GP"], c["IT"]
    HD = 128
    NCH = 512
    nc = bass.Bass("TRN2", target_bir_lowering=False, debug=False)

    ap = lambda t: t.ap()
    x_own = ap(nc.dram_tensor("x_own", [OWN, D], F32, kind="ExternalInput"))
    x_full = ap(nc.dram_tensor("x_full", [S, D], F32, kind="ExternalInput"))
    NPASS = (NKVH + 1) // 2
    PW = min(2, NKVH) * 256
    w_q = ap(nc.dram_tensor("w_q", [NQH, 128, HT, 128], F32, kind="ExternalInput"))
    w_kv = ap(nc.dram_tensor("w_kv", [NPASS, 128, HT, PW], F32, kind="ExternalInput"))
    w_o = ap(nc.dram_tensor("w_o", [HT, 128, HT, 128], F32, kind="ExternalInput"))
    w_guT = ap(nc.dram_tensor("w_guT", [IT, 128, 2 * HT * 128], F32, kind="ExternalInput"))
    w_dT = ap(nc.dram_tensor("w_dT", [HT, 128, IT, 128], F32, kind="ExternalInput"))
    identb_d = ap(nc.dram_tensor("identb", [128, 128], BF16, kind="ExternalInput"))
    identf_d = ap(nc.dram_tensor("identf", [128, 128], F32, kind="ExternalInput"))
    onesc_d = ap(nc.dram_tensor("onesc", [128, 1], F32, kind="ExternalInput"))
    onesr_d = ap(nc.dram_tensor("onesr", [1, 128], F32, kind="ExternalInput"))
    onesb_d = ap(nc.dram_tensor("onesb", [128, 1], BF16, kind="ExternalInput"))
    cosd_d = ap(nc.dram_tensor("cosd", [128, S], F32, kind="ExternalInput"))
    sneg_d = ap(nc.dram_tensor("sneg", [64, S], F32, kind="ExternalInput"))
    spos_d = ap(nc.dram_tensor("spos", [64, S], F32, kind="ExternalInput"))
    cosq_d = ap(nc.dram_tensor("cosq", [128, OWN], F32, kind="ExternalInput"))
    snegq_d = ap(nc.dram_tensor("snegq", [64, OWN], F32, kind="ExternalInput"))
    sposq_d = ap(nc.dram_tensor("sposq", [64, OWN], F32, kind="ExternalInput"))
    masks_d = ap(nc.dram_tensor("masks", [SL * 4, 128, 512], BF16, kind="ExternalInput"))
    out_d = ap(nc.dram_tensor("out_own", [OWN, D], F32, kind="ExternalOutput"))

    xnT_own_d = ap(nc.dram_tensor("xnT_own_d", [HT, 128, OWN], BF16, kind="Internal"))
    xnT_full_d = ap(nc.dram_tensor("xnT_full_d", [HT, 128, S], BF16, kind="Internal"))
    q_d = ap(nc.dram_tensor("q_d", [NKVH, SL, GP, 128, 128], BF16, kind="Internal"))
    attnT_d = ap(nc.dram_tensor("attnT_d", [HT, 128, OWN], BF16, kind="Internal"))
    y_d = ap(nc.dram_tensor("y_d", [HT, 128, OWN], F32, kind="Internal"))


    kb = KB()
    es = contextlib.ExitStack()

    def sb(name, shape, dt):
        return es.enter_context(nc.sbuf_tensor(name, shape, dt))

    def ps(name, shape, dt):
        return es.enter_context(nc.psum_tensor(name, shape, dt))

    AR = sb("AR", [128, 65536], BF16)
    ARF = AR[:, :].bitcast(F32)
    U1 = sb("U1", [128, 16384], BF16)
    U1F = U1[:, :].bitcast(F32)
    U2 = sb("U2", [128, 8192], BF16)
    U2F = U2[:, :].bitcast(F32)
    U3 = sb("U3", [128, 6144], BF16)
    U3F = U3[:, :].bitcast(F32)
    identb = sb("identb_s", [128, 128], BF16)
    identf = sb("identf_s", [128, 128], F32)
    onesc = sb("onesc_s", [128, 1], F32)
    onesr = sb("onesr_s", [1, 128], F32)
    stat6 = [sb(f"stat6{i}", [128, 6 * (D // NCH)], F32) for i in range(2)]
    stat2 = [sb(f"stat2{i}", [128, 2], F32) for i in range(2)]
    rstd = [sb(f"rstd{i}", [128, 1], F32) for i in range(2)]
    negmr = [sb(f"negmr{i}", [128, 1], F32) for i in range(2)]
    recip = sb("recip", [1, 1024], F32)
    bcs = sb("bcs", [128, 512], F32)

    PM = ps("PM", [128, 2048], F32)
    pm = [PM[:, i * 512:(i + 1) * 512] for i in range(4)]
    ptr = ps("ptr", [128, 2048], BF16)
    ptrF = [ptr[:, 0:1024].bitcast(F32), ptr[:, 1024:2048].bitcast(F32)]
    pl = ps("pl", [128, 512], F32)
    pbc = ps("pbc", [128, 512], F32)

    SC = np.float32(1.0 / math.sqrt(HD))
    EB = np.float32(-16.0)

    for dst, src in ((identb, identb_d), (identf, identf_d), (onesc, onesc_d),
                     (onesr, onesr_d)):
        kb.dma("sync", (lambda d=dst, s=src: lambda e: e.dma_start(out=d[:, :], in_=s))(), "dC")
    vC = kb.val("dC")

    AEND = 32768
    xbuf = [U1F[:, 0:D], U1F[:, D:2 * D],
            ARF[:, AEND - 2 * D:AEND - D], ARF[:, AEND - D:AEND]]
    NXB = 4
    xnbuf = [U2[:, 0:D], U2[:, D:2 * D]]
    xnT_own = AR[:, 0:HT * OWN].rearrange("p (h r) -> p h r", h=HT)
    PMB = PM[:, :].bitcast(BF16)
    LN_SLOTS = [(ptr, 0), (ptr, 1024), (PMB, 0), (PMB, 1024), (PMB, 2048), (PMB, 3072)]
    kb._ptr_evac = {i: (("sES" if i % 2 == 0 else "vT"), 0) for i in range(6)}
    kb._ln_slot_rr = [0]

    def ln_pass(ntiles, src_of, dst_of, store_of, tag):
        nch = D // NCH
        state = {}
        xg = kb._lnp_xg if hasattr(kb, "_lnp_xg") else {i: ("sS", 0) for i in range(NXB)}
        tpg = kb._lnp_tpg if hasattr(kb, "_lnp_tpg") else {0: 0, 1: 0}
        for t in range(ntiles + 1):
            if t >= 1:
                u = t - 1
                par = u % 2
                xpar = u % NXB
                kb.wait("vector", "vS", state.pop(u))
                vagg = kb.op("vector",
                             (lambda par=par: lambda e: e.bn_aggr(stat2[par][:, :],
                                                                  stat6[par][:, :]))(),
                             inc="vS")
                kb.wait("scalar", "vS", vagg)
                s1 = kb.op("scalar",
                           (lambda par=par: lambda e: e.activation(
                               rstd[par][:, :], stat2[par][:, 1:2], AF.Ln,
                               bias=np.float32(c["eps"])))(), inc="sS")
                kb.wait("scalar", "sS", s1)
                s2 = kb.op("scalar",
                           (lambda par=par: lambda e: e.activation(
                               rstd[par][:, :], rstd[par][:, :], AF.Exp,
                               scale=np.float32(-0.5)))(), inc="sS")
                kb.wait("vector", "sS", s2)
                v2 = kb.op("vector",
                           (lambda par=par: lambda e: e.tensor_scalar(
                               negmr[par][:, :], stat2[par][:, 0:1],
                               rstd[par][:, 0:1], -1.0, ALU.mult, ALU.mult))(),
                           inc="vS")
                kb.wait("scalar", "vS", v2)
                kb.wait("scalar", "tS", tpg[par])
                hready = {}
                HHF = max(1, HT // 2)
                for hf in range(0, HT, HHF):
                    r = kb.op("scalar",
                              (lambda par=par, xpar=xpar, hf=hf, HHF=HHF: lambda e: e.activation(
                                  xnbuf[par][:, hf * 128:(hf + HHF) * 128],
                                  xbuf[xpar][:, hf * 128:(hf + HHF) * 128], AF.Identity,
                                  bias=negmr[par][:, 0:1],
                                  scale=rstd[par][:, 0:1]))(), inc="sS")
                    for h2 in range(hf, hf + HHF):
                        hready[h2] = r
                xg[xpar] = ("sS", r)
            if t < ntiles:
                xpar2 = t % NXB
                par2 = t % 2
                s_, v_ = xg[xpar2]
                kb.wait("sync", s_, v_)
                dv = kb.dma("sync",
                            (lambda t=t, xpar2=xpar2: lambda e: e.dma_start(
                                out=xbuf[xpar2][:, :], in_=src_of(t)))(),
                            f"dX{xpar2}")
                kb.wait("vector", f"dX{xpar2}", dv)
                for ch in range(nch):
                    kb.op("vector",
                          (lambda ch=ch, par2=par2, xpar2=xpar2: lambda e: e.bn_stats(
                              stat6[par2][:, ch * 6:(ch + 1) * 6],
                              xbuf[xpar2][:, ch * NCH:(ch + 1) * NCH]))(),
                          inc="vS" if ch == nch - 1 else None)
                state[t] = kb.val("vS")
            if t >= 1:
                GW = min(8, HT)
                dfn = dst_of(u)
                evs = {}
                for ht0 in range(0, HT, GW):
                    slot = kb._ln_slot_rr[0] % 6
                    kb._ln_slot_rr[0] += 1
                    sbase, soff = LN_SLOTS[slot]
                    esem, eval_ = kb._ptr_evac[slot]
                    kb.wait("tensor", esem, eval_)
                    kb.wait("tensor", "dC", vC)
                    for k in range(GW):
                        ht = ht0 + k
                        kb.wait("tensor", "sS", hready[ht])
                        kb.op("tensor",
                              (lambda ht=ht, k=k, par=par, sbase=sbase, soff=soff: lambda e: e.transpose(
                                  sbase[:, soff + k * 128: soff + (k + 1) * 128],
                                  xnbuf[par][:, ht * 128:(ht + 1) * 128], identb[:, :]))(),
                              inc="tS" if k == GW - 1 else None)
                    tv = kb.val("tS")
                    eng, sem = ("scalar", "sES") if slot % 2 == 0 else ("vector", "vT")
                    kb.wait(eng, "tS", tv)
                    ev = kb.op(eng,
                               (lambda dfn=dfn, ht0=ht0, eng=eng, GW=GW, sbase=sbase, soff=soff: lambda e: (
                                   e.copy if eng == "scalar" else e.tensor_copy)(
                                   dfn(ht0),
                                   sbase[:, soff: soff + GW * 128].rearrange(
                                       "p (a b) -> p a b", a=GW)))(),
                               inc=sem)
                    kb._ptr_evac[slot] = (sem, ev)
                    evs[sem] = ev
                tpg[par] = kb.val("tS")
                store_of(u, evs)
        kb._lnp_xg = xg
        kb._lnp_tpg = tpg

    def pa_store(u, evs):
        for sem, v in evs.items():
            kb.wait("gpsimd", sem, v)
        kb.dma("gpsimd",
               (lambda u=u: lambda e: e.dma_start(
                   out=xnT_own_d.rearrange("h p r -> p h r")[:, :, u * 128:(u + 1) * 128],
                   in_=xnT_own[:, :, u * 128:(u + 1) * 128]))(),
               "dXO")

    ln_pass(OWN // 128,
            lambda t: x_own[t * 128:(t + 1) * 128, :],
            lambda u: (lambda ht0: xnT_own[:, ht0:ht0 + min(8, HT), u * 128:(u + 1) * 128]),
            pa_store, "pa")

    nst = 4
    stg3 = [AR[:, 32768 + i * HT * 128: 32768 + (i + 1) * HT * 128]
            .rearrange("p (h r) -> p h r", h=HT) for i in range(nst)]
    kb._pb_st = {i: 0 for i in range(nst)}

    def pb_dst(u):
        st_ = u % nst
        kb.wait("scalar", f"dXF{st_}", kb._pb_st[st_])
        kb.wait("vector", f"dXF{st_}", kb._pb_st[st_])
        return lambda ht0: stg3[st_][:, ht0:ht0 + min(8, HT), :]

    def pb_store(u, evs):
        st_ = u % nst
        for sem, v in evs.items():
            kb.wait("gpsimd", sem, v)
        kb._pb_st[st_] = kb.dma("gpsimd",
               (lambda u=u, st_=st_: lambda e: e.dma_start(
                   out=xnT_full_d.rearrange("h p t -> p h t")[:, :, u * 128:(u + 1) * 128],
                   in_=stg3[st_][:, :, :]))(),
               f"dXF{st_}")

    ln_pass(ST,
            lambda t: x_full[t * 128:(t + 1) * 128, :],
            pb_dst, pb_store, "pb")
    kb.barrier()
    if phases <= 1:
        return _finish(nc, kb, es)

    wq = [U1[:, i * (HT * 128):(i + 1) * (HT * 128)] for i in range(2)]
    cosq = U2F[:, 0:OWN]
    snegq = U2F[0:64, OWN:OWN + OWN]
    sposq = U2F[0:64, 2 * OWN:3 * OWN]
    t1b = U3F[:, 0:512]
    t2b = U3F[:, 512:1024]
    qpiece = U3[:, 2048:2048 + OWN]

    kb.dma("sync", lambda e: e.dma_start(out=cosq[:, :], in_=cosq_d), "dT")
    kb.dma("sync", lambda e: e.dma_start(out=snegq[:, :], in_=snegq_d), "dT")
    kb.dma("sync", lambda e: e.dma_start(out=sposq[:, :], in_=sposq_d), "dT")
    vT0 = kb.val("dT")

    def rope(psrc, cos_ap, sneg_ap, spos_ap, dst_ap, deps, tag):
        N = dst_ap.shape[-1]
        for s, v in deps:
            kb.wait("vector", s, v)
        kb.wait("vector", "vS", getattr(kb, "_rope_war", 0))
        kb.op("vector", lambda e: e.tensor_tensor(t1b[:, 0:N], psrc[:, 0:N], cos_ap, ALU.mult), inc="vS")
        kb.op("vector", lambda e: e.tensor_tensor(t2b[0:64, 0:N], psrc[64:128, 0:N], sneg_ap, ALU.mult), inc="vS")
        kb.op("vector", lambda e: e.tensor_tensor(t2b[64:128, 0:N], psrc[0:64, 0:N], spos_ap, ALU.mult), inc="vS")
        v4 = kb.val("vS")
        kb.wait("vector", "vS", v4)
        r = kb.op("vector", lambda e: e.tensor_tensor(dst_ap, t1b[:, 0:N], t2b[:, 0:N], ALU.add), inc="vS")
        kb._rope_war = r
        return r

    qs_done = 0
    for h in range(NQH):
        par = h % 2
        if h >= 2:
            kb.wait("gpsimd", "tS", kb._wq_used[par])
        if not hasattr(kb, "_wq_used"):
            kb._wq_used = {}
        dv = kb.dma("gpsimd",
                    (lambda h=h, par=par: lambda e: e.dma_start(
                        out=wq[par].rearrange("p (h c) -> p h c", h=HT),
                        in_=w_q[h]))(),
                    f"dWQ{par}")
        rope_done = None
        for rb in range(NRB):
            bank = pm[rb % 2]
            kb.wait("tensor", f"dWQ{par}", dv)
            kb.wait("tensor", "ph", kb.val("ph"))
            if hasattr(kb, "_pcq_bank") and (rb % 2) in kb._pcq_bank:
                kb.wait("tensor", "vS", kb._pcq_bank[rb % 2])
            for ht in range(HT):
                mm = kb.op("tensor",
                           (lambda ht=ht, rb=rb, bank=bank, par=par: lambda e: e.matmul(
                               bank[:, 0:RB], wq[par][:, ht * 128:(ht + 1) * 128],
                               xnT_own[:, ht, rb * RB:(rb + 1) * RB],
                               start=(ht == 0), stop=(ht == HT - 1)))(),
                           inc="tS" if ht == HT - 1 else None)
            if not hasattr(kb, "_wq_used"):
                kb._wq_used = {}
            kb._wq_used[par] = kb.val("tS")
            deps = [("tS", kb.val("tS"))]
            if rb == 0 and qs_done:
                deps.append(("dQ", qs_done))
            rope_done = rope(bank,
                             cosq[:, rb * RB:(rb + 1) * RB],
                             snegq[0:64, rb * RB:(rb + 1) * RB],
                             sposq[0:64, rb * RB:(rb + 1) * RB],
                             qpiece[:, rb * RB:(rb + 1) * RB],
                             deps + [("dT", vT0)], f"q{h}")
            if not hasattr(kb, "_pcq_bank"):
                kb._pcq_bank = {}
            kb._pcq_bank[rb % 2] = kb.val("vS")
        g, hh = h // GP, h % GP
        kb.wait("sync", "vS", rope_done)
        qs_done = kb.dma("sync",
                         (lambda g=g, hh=hh: lambda e: e.dma_start(
                             out=q_d[g, :, hh, :, :].rearrange("s p q -> p s q"),
                             in_=qpiece[:, :].rearrange("p (s q) -> p s q", q=128)))(),
                         "dQ")
    kb.barrier()
    if phases <= 2:
        return _finish(nc, kb, es)

    Kc = AR[:, 0:NKVH * S]
    Vc = AR[:, 32768:32768 + NKVH * S]
    wkv = U1[:, 0:HT * 512]
    NCK = 4
    xnf = [U2[:, i * 2048:(i + 1) * 2048] for i in range(NCK)]
    ctb = [U3F[:, 1024:1536], U3F[:, 1024:1536]]
    sntb = [U3F[0:64, 1536:2048], U3F[0:64, 1536:2048]]
    sptb = [U3F[0:64, 2048:2560], U3F[0:64, 2048:2560]]
    vtmp = U3[:, 5120:5632]

    kb._pd_bankg = {u: ("tS", 0) for u in range(4)}
    kb._ptr_evac = {0: 0, 1: 0}
    for pp in range(NPASS):
        g0 = 2 * pp
        ng = min(2, NKVH - g0)
        kb.wait("gpsimd", "tS", kb.val("tS"))
        HQ = max(1, HT // 4)
        dwv = {}
        for qtr in range((HT + HQ - 1) // HQ):
            h0, h1 = qtr * HQ, min(HT, (qtr + 1) * HQ)
            dwv[qtr] = kb.dma("gpsimd",
                        (lambda pp=pp, ng=ng, h0=h0, h1=h1: lambda e: e.dma_start(
                            out=wkv.rearrange("p (h c) -> p h c", h=HT)[:, h0:h1, 0:ng * 256],
                            in_=w_kv[pp][:, h0:h1, 0:ng * 256]))(),
                        f"dWKV{qtr}")
        for tb in range(S // 512):
            tpar = tb % 2
            kb.wait("scalar", "vS", kb._pd_trig_g if hasattr(kb, "_pd_trig_g") else 0)
            for dst, src in ((ctb[tpar], cosd_d[:, tb * 512:(tb + 1) * 512]),
                             (sntb[tpar], sneg_d[:, tb * 512:(tb + 1) * 512]),
                             (sptb[tpar], spos_d[:, tb * 512:(tb + 1) * 512])):
                kb.dma("scalar", (lambda d=dst, s=src: lambda e: e.dma_start(out=d, in_=s))(), "dTT")
            vtrig = kb.val("dTT")
            nck = (HT + 3) // 4
            for ck in range(nck):
                cpar = (tb * nck + ck) % NCK
                nht = min(4, HT - ck * 4)
                if hasattr(kb, "_xnf_used") and cpar in kb._xnf_used:
                    kb.wait("sync", "tS", kb._xnf_used[cpar])
                dv = kb.dma("sync",
                            (lambda ck=ck, cpar=cpar, tb=tb, nht=nht: lambda e: e.dma_start(
                                out=xnf[cpar].rearrange("p (a t) -> p a t", a=4)[:, 0:nht, :],
                                in_=xnT_full_d.rearrange("h p t -> p h t")[:, ck * 4:ck * 4 + nht,
                                                                           tb * 512:(tb + 1) * 512]))(),
                            f"dXN{cpar}")
                kb.wait("tensor", f"dXN{cpar}", dv)
                if ck == 0:
                    for u, (semn, valn) in kb._pd_bankg.items():
                        kb.wait("tensor", semn, valn)
                for hl in range(nht):
                    ht = ck * 4 + hl
                    kb.wait("tensor", f"dWKV{ht // HQ}", dwv[ht // HQ])
                    for u in range(2 * ng):
                        mm = kb.op("tensor",
                                   (lambda u=u, ht=ht, hl=hl, cpar=cpar: lambda e: e.matmul(
                                       pm[u][:, :], wkv[:, ht * 512 + u * 128: ht * 512 + (u + 1) * 128],
                                       xnf[cpar][:, hl * 512:(hl + 1) * 512],
                                       start=(ht == 0), stop=(ht == HT - 1)))(),
                                   inc="tS" if (hl == nht - 1 and u == 2 * ng - 1) else None)
                if not hasattr(kb, "_xnf_used"):
                    kb._xnf_used = {}
                kb._xnf_used[cpar] = kb.val("tS") + (0 if ck < nck - 1 else 0)
            tmm = kb.val("tS")
            for gl in range(ng):
                g = g0 + gl
                rope_done = rope(pm[2 * gl],
                                 ctb[tpar][:, :], sntb[tpar][0:64, :], sptb[tpar][0:64, :],
                                 Kc[:, g * S + tb * 512: g * S + (tb + 1) * 512],
                                 [("tS", tmm), ("dTT", vtrig)], f"k{g}t{tb}")
                kb._pd_bankg[2 * gl] = ("vS", rope_done)
                kb._pd_trig_g = rope_done
                kb.wait("scalar", "tS", tmm)
                if hasattr(kb, "_vtmp_used"):
                    kb.wait("scalar", "tS", kb._vtmp_used)
                sc = kb.op("scalar",
                           (lambda gl=gl: lambda e: e.copy(vtmp[:, :], pm[2 * gl + 1][:, :]))(),
                           inc="sS")
                kb._pd_bankg[2 * gl + 1] = ("sS", sc)
                slot = tb % 2
                kb.wait("tensor", "sS", sc)
                kb.wait("tensor", "vT", kb._ptr_evac[slot])
                for i in range(4):
                    kb.op("tensor",
                          (lambda i=i, slot=slot: lambda e: e.transpose(
                              ptr[:, slot * 1024 + i * 128: slot * 1024 + (i + 1) * 128],
                              vtmp[:, i * 128:(i + 1) * 128], identb[:, :]))(),
                          inc="tS" if i == 3 else None)
                kb._vtmp_used = kb.val("tS")
                kb.wait("vector", "tS", kb.val("tS"))
                ev = kb.op("vector",
                           (lambda g=g, tb=tb, slot=slot: lambda e: e.tensor_copy(
                               Vc[:, g * S + tb * 512: g * S + (tb + 1) * 512],
                               ptr[:, slot * 1024: slot * 1024 + 512]))(),
                           inc="vT")
                kb._ptr_evac[slot] = ev
    kb.barrier()
    if phases <= 3:
        return _finish(nc, kb, es)

    QGW = SL * GP * 128
    qg = [U2[:, i * QGW:(i + 1) * QGW] for i in range(2)]
    maskbuf = U1[:, 0:SL * 4 * 512]
    pbuf = [U3[:, i * 1024:(i + 1) * 1024] for i in range(4)]
    stage = [U3[:, 4096 + i * 512:4096 + (i + 1) * 512] for i in range(2)]
    onesb = U3[:, 5120:5136]
    dm = kb.dma("sync", lambda e: e.dma_start(
        out=maskbuf[:, :].rearrange("p (m q) -> p m q", q=512),
        in_=masks_d.rearrange("m p q -> p m q")), "dM")
    kb.dma("sync", lambda e: e.dma_start(out=onesb[:, 0:1], in_=onesb_d), "dM")
    dm = kb.val("dM")

    kb._pe_pv_guard = {0: 0, 1: 0}
    kb._pe_pl_guard = {0: 0, 1: 0}
    kb._pe_pbc_guard = 0
    kb._pe_sc_guard = {0: 0, 1: 0}
    kb._pe_p_guard = {i: (0, 0) for i in range(4)}
    kb._pe_stage_guard = {0: 0, 1: 0}
    qload = {}

    def emit_scores(lane, g, s, par, bb, Tn):
        pb_ = 2 * lane + (bb % 2)
        tts = (2 * bb, 2 * bb + 1)
        kb.wait("tensor", qload[g][0], qload[g][1])
        kb.wait("tensor", "sS", kb._pe_sc_guard[lane])
        if bb == 0:
            kb.wait("tensor", "dM", dm)
            kb.wait("tensor", "vS", kb._pe_pv_guard[lane])
            kb.wait("tensor", "vS", kb._pe_pl_guard[lane])
        for k, tt in enumerate(tts):
            smm = kb.op("tensor",
                        (lambda g=g, s=s, tt=tt, lane=lane, k=k, par=par: lambda e: e.matmul(
                            PM[:, lane * 1024 + k * 512: lane * 1024 + (k + 1) * 512],
                            Kc[:, g * S + tt * 128:g * S + (tt + 1) * 128],
                            qg[par][:, s * GP * 128:(s + 1) * GP * 128],
                            start=True, stop=True))(),
                        inc="tS" if k == 1 else None)
        kb.wait("scalar", "tS", smm)
        kb.wait("scalar", "tS", kb._pe_p_guard[pb_][0])
        kb.wait("scalar", "vS", kb._pe_p_guard[pb_][1])
        ex = kb.op("scalar",
                   (lambda lane=lane, pb_=pb_: lambda e: e.activation(
                       pbuf[pb_][:, :], PM[:, lane * 1024:(lane + 1) * 1024],
                       AF.Exp, bias=EB, scale=SC))(),
                   inc="sS")
        kb._pe_sc_guard[lane] = ex
        mg = 0
        for k, tt in enumerate(tts):
            dt = tt - (Tn - 4)
            if dt >= 0:
                kb.wait("vector", "sS", ex)
                kb.wait("vector", "dM", dm)
                kb.op("vector",
                      (lambda s=s, dt=dt, pb_=pb_, k=k: lambda e: e.tensor_tensor(
                          pbuf[pb_][:, k * 512:(k + 1) * 512],
                          pbuf[pb_][:, k * 512:(k + 1) * 512],
                          maskbuf[:, (s * 4 + dt) * 512:(s * 4 + dt + 1) * 512],
                          ALU.mult))(),
                      inc="vS")
                mg = kb.val("vS")
        return (tts, pb_, mg, ex)

    def emit_pvl(lane, g, Tn, pend):
        tts, pb_, mg, ex = pend
        if mg:
            kb.wait("tensor", "vS", mg)
        else:
            kb.wait("tensor", "sS", ex)
        lm = 0
        for k, tt in enumerate(tts):
            kb.op("tensor",
                  (lambda g=g, tt=tt, pb_=pb_, k=k, Tn=Tn, lane=lane: lambda e: e.matmul(
                      ptrF[lane][:, :], Vc[:, g * S + tt * 128:g * S + (tt + 1) * 128],
                      pbuf[pb_][:, k * 512:(k + 1) * 512],
                      start=(tt == 0), stop=(tt == Tn - 1)))(),
                  inc=None)
            lm = kb.op("tensor",
                       (lambda tt=tt, pb_=pb_, k=k, Tn=Tn, lane=lane: lambda e: e.matmul(
                           pl[64 * lane:64 * lane + 1, :], onesb[:, 0:1],
                           pbuf[pb_][:, k * 512:(k + 1) * 512],
                           start=(tt == 0), stop=(tt == Tn - 1)))(),
                       inc="tS")
        kb._pe_p_guard[pb_] = (lm, mg)

    def emit_tail(lane, g, s):
        pv_done = kb.val("tS")
        kb.wait("vector", "tS", pv_done)
        rc = kb.op("vector",
                   (lambda lane=lane: lambda e: e.reciprocal(
                       recip[0:1, lane * 512:(lane + 1) * 512],
                       pl[64 * lane:64 * lane + 1, :]))(), inc="vS")
        kb._pe_pl_guard[lane] = rc
        kb.wait("tensor", "vS", rc)
        kb.wait("tensor", "sS", kb._pe_pbc_guard)
        bc = kb.op("tensor",
                   (lambda lane=lane: lambda e: e.matmul(
                       pbc[:, :], onesr[:, :], recip[0:1, lane * 512:(lane + 1) * 512],
                       start=True, stop=True))(), inc="tS")
        kb.wait("scalar", "tS", bc)
        cp = kb.op("scalar", lambda e: e.copy(bcs[:, :], pbc[:, :]), inc="sS")
        kb._pe_pbc_guard = cp
        spar = (g * SL + s) % 2
        kb.wait("vector", "sS", cp)
        kb.wait("vector", f"dA{spar}", kb._pe_stage_guard[spar])
        dv = kb.op("vector",
                   (lambda spar=spar, lane=lane: lambda e: e.tensor_tensor(
                       stage[spar][:, :], ptrF[lane][:, :], bcs[:, :], ALU.mult))(),
                   inc="vS")
        kb._pe_pv_guard[lane] = dv
        kb.wait("sync", "vS", dv)
        stv = kb.dma("sync",
                     (lambda g=g, s=s, spar=spar: lambda e: e.dma_start(
                         out=attnT_d.rearrange("h p r -> p h r")[:, g * GP:(g + 1) * GP,
                                                                 s * 128:(s + 1) * 128],
                         in_=stage[spar][:, :].rearrange("p (h q) -> p h q", h=GP)))(),
                     f"dA{spar}")
        kb._pe_stage_guard[spar] = stv

    for g in range(NKVH):
        par = g % 2
        if g not in qload:
            if g >= 2:
                kb.wait("sync", "tS", kb._qg_used[par])
            qload[g] = (f"dQG{par}", kb.dma("sync",
                              (lambda g=g, par=par: lambda e: e.dma_start(
                                  out=qg[par][:, :].rearrange("p (s h q) -> p s h q", s=SL, h=GP),
                                  in_=q_d[g].rearrange("s h p q -> p s h q")))(),
                              f"dQG{par}"))
        if g + 1 < NKVH:
            par2 = (g + 1) % 2
            if g + 1 >= 2:
                kb.wait("sync", "tS", kb._qg_used.get(par2, 0) if hasattr(kb, "_qg_used") else 0)
            qload[g + 1] = (f"dQG{par2}", kb.dma("sync",
                                  (lambda g=g + 1, par=par2: lambda e: e.dma_start(
                                      out=qg[par][:, :].rearrange("p (s h q) -> p s h q", s=SL, h=GP),
                                      in_=q_d[g].rearrange("s h p q -> p s h q")))(),
                                  f"dQG{par2}"))
        for j in range(SL // 2):
            sA, sB = 2 * j, 2 * j + 1
            TA, TB = c["T"][sA], c["T"][sB]
            pend = {0: None, 1: None}
            for bb in range(TA // 2):
                pend_new_a = emit_scores(0, g, sA, par, bb, TA)
                if pend[1] is not None:
                    emit_pvl(1, g, TB, pend[1])
                    pend[1] = None
                if bb < TB // 2:
                    pend[1] = emit_scores(1, g, sB, par, bb, TB)
                if pend[0] is not None:
                    emit_pvl(0, g, TA, pend[0])
                pend[0] = pend_new_a
            if pend[1] is not None:
                emit_pvl(1, g, TB, pend[1])
            if pend[0] is not None:
                emit_pvl(0, g, TA, pend[0])
            emit_tail(0, g, sA)
            emit_tail(1, g, sB)
        if not hasattr(kb, "_qg_used"):
            kb._qg_used = {}
        kb._qg_used[par] = kb.val("tS")
    kb.barrier()
    if phases <= 4:
        return _finish(nc, kb, es)

    attnT = AR[:, 0:HT * OWN].rearrange("p (h r) -> p h r", h=HT)
    wo = [U1[:, i * (HT * 128):(i + 1) * (HT * 128)] for i in range(2)]
    ybuf = [U2F[:, i * 512:(i + 1) * 512] for i in range(2)]
    kb.dma("sync", lambda e: e.dma_start(
        out=attnT[:, :, :], in_=attnT_d.rearrange("h p r -> p h r")), "dAT")
    vat = kb.val("dAT")
    kb._pf_w_used = {}
    kb._pf_bank = {}
    kb._pf_ybuf = {0: 0, 1: 0}
    for of in range(HT):
        par = of % 2
        if of >= 2:
            kb.wait("gpsimd", "tS", kb._pf_w_used[par])
        dw = kb.dma("gpsimd",
                    (lambda of=of, par=par: lambda e: e.dma_start(
                        out=wo[par].rearrange("p (h c) -> p h c", h=HT),
                        in_=w_o[of]))(),
                    f"dWO{par}")
        for rb in range(NRB):
            bank = pm[(of * NRB + rb) % 4]
            kb.wait("tensor", f"dWO{par}", dw)
            kb.wait("tensor", "dAT", vat)
            bg = kb._pf_bank.get((of * NRB + rb) % 4)
            if bg:
                kb.wait("tensor", "vS", bg)
            for ht in range(HT):
                kb.op("tensor",
                      (lambda ht=ht, rb=rb, bank=bank, par=par: lambda e: e.matmul(
                          bank[:, 0:RB], wo[par][:, ht * 128:(ht + 1) * 128],
                          attnT[:, ht, rb * RB:(rb + 1) * RB],
                          start=(ht == 0), stop=(ht == HT - 1)))(),
                      inc="tS" if ht == HT - 1 else None)
            kb._pf_w_used[par] = kb.val("tS")
            ypar = (of * NRB + rb) % 2
            kb.wait("vector", "tS", kb.val("tS"))
            kb.wait("vector", f"dY{ypar}", kb._pf_ybuf[ypar])
            ev = kb.op("vector",
                       (lambda bank=bank, ypar=ypar: lambda e: e.tensor_copy(
                           ybuf[ypar][:, 0:RB], bank[:, 0:RB]))(),
                       inc="vS")
            kb._pf_bank[(of * NRB + rb) % 4] = ev
            kb.wait("sync", "vS", ev)
            sy = kb.dma("sync",
                        (lambda of=of, rb=rb, ypar=ypar: lambda e: e.dma_start(
                            out=y_d[of, :, rb * RB:(rb + 1) * RB],
                            in_=ybuf[ypar][:, 0:RB]))(),
                        f"dY{ypar}")
            kb._pf_ybuf[ypar] = sy
    kb.barrier()
    if phases <= 5:
        return _finish(nc, kb, es)

    xnA = AR[:, 0:HT * OWN].rearrange("p (h r) -> p h r", h=HT)
    nsb2 = min(8, IT)
    base2 = IT // nsb2
    SBS2 = [base2 + (1 if i < IT % nsb2 else 0) for i in range(nsb2)]
    SBmax = max(SBS2)
    inter2 = [AR[:, HT * OWN + i * SBmax * OWN: HT * OWN + (i + 1) * SBmax * OWN]
              .rearrange("p (i r) -> p i r", r=OWN) for i in range(2)]
    wgu = [U1[:, i * (2 * HT * 128):(i + 1) * (2 * HT * 128)] for i in range(2)]
    NWD = 2
    wd = [U2[:, i * (2 * SBmax * 128):(i + 1) * (2 * SBmax * 128)] for i in range(NWD)]
    sg = [U2F[:, 3072 + i * 512:3072 + (i + 1) * 512] for i in range(2)]
    fbuf = [U3F[:, i * OWN:(i + 1) * OWN] for i in range(2)]
    kb.dma("sync", lambda e: e.dma_start(
        out=xnA[:, :, :], in_=xnT_own_d.rearrange("h p r -> p h r")), "dXA")
    vxa = kb.val("dXA")
    kb._pg_w_used = {0: 0, 1: 0}
    kb._pg_sg = {0: 0, 1: 0}
    kb._pg_fbuf = {0: 0, 1: 0}
    kb._pg_islot = {0: 0, 1: 0}
    kb._pg_fbank = {0: 0, 1: 0}
    kb._pg_wd_used = {0: 0, 1: 0, 2: 0}
    sb_starts = []
    t = 0
    for n_ in SBS2:
        sb_starts.append(t)
        t += n_

    def emit_gateup(sbi):
        it0 = sb_starts[sbi]
        sbn = SBS2[sbi]
        islot = sbi % 2
        inter_slot = inter2[islot]
        kb.wait("tensor", "tS", kb._pg_islot[islot])
        for itl in range(sbn):
            it = it0 + itl
            par = it % 2
            kb.wait("gpsimd", "tS", kb._pg_w_used[par])
            dgv = kb.dma("gpsimd",
                         (lambda it=it, par=par: lambda e: e.dma_start(
                             out=wgu[par][:, :], in_=w_guT[it]))(),
                         f"dWG{par}")
            for rb in range(NRB):
                bg_ = pm[rb % 2]
                bu_ = pm[2 + rb % 2]
                kb.wait("tensor", f"dWG{par}", dgv)
                kb.wait("tensor", "dXA", vxa)
                kb.wait("tensor", "sS", kb._pg_gate_guard.get(rb % 2, 0) if hasattr(kb, "_pg_gate_guard") else 0)
                kb.wait("tensor", "vS", kb._pg_up_guard.get(rb % 2, 0) if hasattr(kb, "_pg_up_guard") else 0)
                for ht in range(HT):
                    kb.op("tensor",
                          (lambda ht=ht, rb=rb, bg_=bg_, par=par: lambda e: e.matmul(
                              bg_[:, 0:RB], wgu[par][:, ht * 128:(ht + 1) * 128],
                              xnA[:, ht, rb * RB:(rb + 1) * RB],
                              start=(ht == 0), stop=(ht == HT - 1)))(),
                          inc="tS" if ht == HT - 1 else None)
                gmm = kb.val("tS")
                for ht in range(HT):
                    kb.op("tensor",
                          (lambda ht=ht, rb=rb, bu_=bu_, par=par: lambda e: e.matmul(
                              bu_[:, 0:RB], wgu[par][:, HT * 128 + ht * 128:HT * 128 + (ht + 1) * 128],
                              xnA[:, ht, rb * RB:(rb + 1) * RB],
                              start=(ht == 0), stop=(ht == HT - 1)))(),
                          inc="tS" if ht == HT - 1 else None)
                umm = kb.val("tS")
                kb._pg_w_used[par] = umm
                spar = rb % 2
                kb.wait("scalar", "tS", gmm)
                kb.wait("scalar", "vS", kb._pg_sg[spar])
                sl_ = kb.op("scalar",
                            (lambda bg_=bg_, spar=spar: lambda e: e.activation(
                                sg[spar][:, 0:RB], bg_[:, 0:RB], AF.Silu))(),
                            inc="sS")
                if not hasattr(kb, "_pg_gate_guard"):
                    kb._pg_gate_guard = {}
                    kb._pg_up_guard = {}
                kb._pg_gate_guard[spar] = sl_
                kb.wait("vector", "tS", umm)
                kb.wait("vector", "sS", sl_)
                iv = kb.op("vector",
                           (lambda itl=itl, rb=rb, bu_=bu_, spar=spar, inter_slot=inter_slot: lambda e: e.tensor_tensor(
                               inter_slot[:, itl, rb * RB:(rb + 1) * RB], bu_[:, 0:RB],
                               sg[spar][:, 0:RB], ALU.mult))(),
                           inc="vS")
                kb._pg_up_guard[spar] = iv
                kb._pg_sg[spar] = iv
        return kb.val("vS")

    def wd_load(sbi, ht0):
        it0 = sb_starts[sbi]
        sbn = SBS2[sbi]
        nh = min(2, HT - ht0)
        par = (ht0 // 2) % NWD
        kb.wait("gpsimd", "tS", kb._pg_wd_used[par])
        return kb.dma("gpsimd",
                     (lambda it0=it0, sbn=sbn, ht0=ht0, nh=nh, par=par: lambda e: e.dma_start(
                         out=wd[par].rearrange("p (u i c) -> p u i c", u=2, c=128)[:, 0:nh, 0:sbn, :],
                         in_=w_dT[ht0:ht0 + nh].rearrange("u p i c -> p u i c")[:, :, it0:it0 + sbn, :]))(),
                     f"dWD{par}")

    def emit_down(sbi, inter_done, pre):
        it0 = sb_starts[sbi]
        sbn = SBS2[sbi]
        islot = sbi % 2
        inter_slot = inter2[islot]
        pend = []
        for ht0 in range(0, HT, 2):
            nh = min(2, HT - ht0)
            par = (ht0 // 2) % NWD
            ddv = pre.get(ht0) or wd_load(sbi, ht0)
            for pfpar, pht, pev in pend:
                kb.wait("gpsimd", "vS", pev)
                kb._pg_fbuf[pfpar] = kb.dma("gpsimd",
                            (lambda ht=pht, fpar=pfpar: lambda e: e.dma_start(
                                out=y_d[ht, :, :],
                                in_=fbuf[fpar][:, 0:OWN], accum_op=ALU.add))(),
                            f"dFA{pfpar}")
            pend = []
            for u in range(nh):
                ht = ht0 + u
                fpar = ht % 2
                kb.wait("vector", f"dFA{fpar}", kb._pg_fbuf[fpar])
                ev = 0
                for rb in range(NRB):
                    bk = (ht * NRB + rb) % 2
                    bank = ptrF[bk]
                    kb.wait("tensor", f"dWD{par}", ddv)
                    kb.wait("tensor", "vS", inter_done)
                    kb.wait("tensor", "vS", kb._pg_fbank[bk])
                    for itl in range(sbn):
                        kb.op("tensor",
                              (lambda itl=itl, rb=rb, bank=bank, u=u, par=par, inter_slot=inter_slot: lambda e: e.matmul(
                                  bank[:, 0:RB], wd[par][:, (u * SBmax + itl) * 128:(u * SBmax + itl + 1) * 128],
                                  inter_slot[:, itl, rb * RB:(rb + 1) * RB],
                                  start=(itl == 0), stop=(itl == sbn - 1)))(),
                              inc="tS" if itl == sbn - 1 else None)
                    kb._pg_wd_used[par] = kb.val("tS")
                    kb.wait("vector", "tS", kb.val("tS"))
                    ev = kb.op("vector",
                               (lambda bank=bank, fpar=fpar, rb=rb: lambda e: e.tensor_copy(
                                   fbuf[fpar][:, rb * RB:(rb + 1) * RB], bank[:, 0:RB]))(),
                               inc="vS")
                    kb._pg_fbank[bk] = ev
                pend.append((fpar, ht, ev))
        for pfpar, pht, pev in pend:
            kb.wait("gpsimd", "vS", pev)
            kb._pg_fbuf[pfpar] = kb.dma("gpsimd",
                        (lambda ht=pht, fpar=pfpar: lambda e: e.dma_start(
                            out=y_d[ht, :, :],
                            in_=fbuf[fpar][:, 0:OWN], accum_op=ALU.add))(),
                        f"dFA{pfpar}")
        kb._pg_islot[islot] = kb.val("tS")

    pend_sb = None
    for sbi in range(nsb2):
        pre = {}
        if pend_sb is not None:
            for ht0 in (0, 2):
                if ht0 < HT:
                    pre[ht0] = wd_load(pend_sb[0], ht0)
        idn = emit_gateup(sbi)
        if pend_sb is not None:
            emit_down(pend_sb[0], pend_sb[1], pre)
        pend_sb = (sbi, idn)
    emit_down(pend_sb[0], pend_sb[1], {})
    kb.barrier()
    if phases <= 6:
        return _finish(nc, kb, es)

    NPH = 3
    yv = [ARF[:, i * HT * 128:(i + 1) * HT * 128] for i in range(NPH)]
    outv = [ARF[:, (NPH + i) * HT * 128:(NPH + 1 + i) * HT * 128] for i in range(NPH)]
    xot = [U1F[:, 0:D], U1F[:, D:2 * D], ARF[:, 2 * NPH * HT * 128:2 * NPH * HT * 128 + D]]
    kb._ph_out_guard = {i: 0 for i in range(NPH)}
    kb._ph_rr = [0]
    kb._ph_yv = {i: 0 for i in range(NPH)}
    kb._ph_xot = {i: 0 for i in range(NPH)}
    for ot in range(OWN // 128):
        par = ot % NPH
        kb.wait("sync", "tS", kb._ph_yv[par])
        dy = kb.dma("sync",
                    (lambda ot=ot, par=par: lambda e: e.dma_start(
                        out=yv[par].rearrange("p (h r) -> p h r", h=HT),
                        in_=y_d.rearrange("h p r -> p h r")[:, :, ot * 128:(ot + 1) * 128]))(),
                    f"dYL{par}")
        kb.wait("sync", "vS", kb._ph_xot[par])
        dx = kb.dma("sync",
                    (lambda ot=ot, par=par: lambda e: e.dma_start(
                        out=xot[par][:, :], in_=x_own[ot * 128:(ot + 1) * 128, :]))(),
                    f"dXL{par}")
        for ht0 in range(0, HT, 4):
            slot6 = kb._ph_rr[0] % 6
            kb._ph_rr[0] += 1
            bank = pm[slot6] if slot6 < 4 else ptrF[slot6 - 4]
            kb.wait("tensor", f"dYL{par}", dy)
            bg = kb._ph_bank.get(slot6) if hasattr(kb, "_ph_bank") else None
            if bg:
                kb.wait("tensor", "vS", bg)
            for k in range(4):
                ht = ht0 + k
                kb.op("tensor",
                      (lambda ht=ht, k=k, bank=bank, par=par: lambda e: e.transpose(
                          bank[:, k * 128:(k + 1) * 128], yv[par][:, ht * 128:(ht + 1) * 128],
                          identf[:, :]))(),
                      inc="tS" if k == 3 else None)
            tv = kb.val("tS")
            kb.wait("vector", "tS", tv)
            kb.wait("vector", f"dXL{par}", dx)
            kb.wait("vector", f"dOUT{par}", kb._ph_out_guard[par])
            av = kb.op("vector",
                       (lambda ht0=ht0, bank=bank, par=par: lambda e: e.tensor_tensor(
                           outv[par][:, ht0 * 128:(ht0 + 4) * 128], bank[:, 0:512],
                           xot[par][:, ht0 * 128:(ht0 + 4) * 128], ALU.add))(),
                       inc="vS")
            if not hasattr(kb, "_ph_bank"):
                kb._ph_bank = {}
            kb._ph_bank[slot6] = av
        kb._ph_yv[par] = kb.val("tS")
        kb._ph_xot[par] = kb.val("vS")
        kb.wait("sync", "vS", av)
        kb._ph_out_guard[par] = kb.dma("sync",
               (lambda ot=ot, par=par: lambda e: e.dma_start(
                   out=out_d[ot * 128:(ot + 1) * 128, :], in_=outv[par][:, :]))(),
               f"dOUT{par}")
    kb.wait("sync", "dOUT0", kb.val("dOUT0"))
    kb.wait("sync", "dOUT1", kb.val("dOUT1"))
    kb.barrier()

    return _finish(nc, kb, es)


def _finish(nc, kb, es):
    semobjs = {}
    for name in kb.sem_names:
        semobjs[name] = es.enter_context(nc.semaphore(name))
    with es:
        with nc.Block() as block:
            block.sync(kb.runner("sync", semobjs))
            block.gpsimd(kb.runner("gpsimd", semobjs))
            block.tensor(kb.runner("tensor", semobjs))
            block.vector(kb.runner("vector", semobjs))
            block.scalar(kb.runner("scalar", semobjs))
    return nc


def rope_tables(cfg, positions):
    HD = 128
    invf = cfg["theta"] ** (-np.arange(0, HD, 2, dtype=np.float64) / HD)
    ang = np.asarray(positions, np.float64)[None, :] * invf[:, None]
    cos = np.cos(ang).astype(np.float32)
    sin = np.sin(ang).astype(np.float32)
    cosd = np.concatenate([cos, cos], 0)
    return cosd, (-sin).astype(np.float32), sin.astype(np.float32)


def perm_even_odd():
    return np.concatenate([np.arange(0, 128, 2), np.arange(1, 128, 2)])


def make_host_inputs(inputs, cfg):
    c = derive(cfg)
    D, S, SL, OWN = c["D"], c["S"], c["SL"], c["OWN"]
    NQH, NKVH = c["NQH"], c["NKVH"]
    HD = 128
    x = np.asarray(inputs["L0_attn_x"], np.float32)
    w_qkv = np.asarray(inputs["L0_attn_w_qkv"], np.float32)
    w_o = np.ascontiguousarray(np.asarray(inputs["L0_attn_w_o"], np.float32))
    w_g = np.asarray(inputs["L0_ffn_w_gate"], np.float32)
    w_u = np.asarray(inputs["L0_ffn_w_up"], np.float32)
    w_dn = np.asarray(inputs["L0_ffn_w_down"], np.float32)

    HT = c["HT"]; IT = c["IT"]
    NPASS = (NKVH + 1) // 2
    PW = min(2, NKVH) * 256
    perm = perm_even_odd()
    wq = w_qkv[:, :D].reshape(D, NQH, HD)[:, :, perm].reshape(D, D)
    wk = w_qkv[:, D:D + NKVH * HD].reshape(D, NKVH, HD)[:, :, perm]
    wv = w_qkv[:, D + NKVH * HD:].reshape(D, NKVH, HD)
    w_kv = np.concatenate([wk, wv], axis=2).reshape(D, NKVH * 256)
    w_q_t = np.ascontiguousarray(
        wq.reshape(HT, 128, NQH, 128).transpose(2, 1, 0, 3))
    w_kv_t = np.ascontiguousarray(
        w_kv.reshape(HT, 128, NPASS, PW).transpose(2, 1, 0, 3))
    w_o_t = np.ascontiguousarray(
        w_o.reshape(HT, 128, HT, 128).transpose(2, 1, 0, 3))
    w_g_t = w_g.reshape(IT, 128, HT, 128).transpose(0, 3, 2, 1)
    w_u_t = w_u.reshape(IT, 128, HT, 128).transpose(0, 3, 2, 1)
    w_gu_t = np.ascontiguousarray(
        np.concatenate([w_g_t.reshape(IT, 128, HT * 128),
                        w_u_t.reshape(IT, 128, HT * 128)], axis=2))
    w_d_t = np.ascontiguousarray(
        w_dn.reshape(HT, 128, IT, 128).transpose(0, 3, 2, 1))

    cosd, sneg, spos = rope_tables(cfg, np.arange(S))
    identb = np.eye(128, dtype=bfdt)
    identf = np.eye(128, dtype=np.float32)
    onesc = np.ones((128, 1), np.float32)
    onesr = np.ones((1, 128), np.float32)

    shared = dict(w_q=w_q_t, w_kv=w_kv_t,
                  w_o=w_o_t, w_guT=w_gu_t, w_dT=w_d_t,
                  identb=identb, identf=identf, onesc=onesc, onesr=onesr,
                  onesb=np.ones((128, 1), bfdt),
                  cosd=np.ascontiguousarray(cosd), sneg=np.ascontiguousarray(sneg),
                  spos=np.ascontiguousarray(spos))

    in_maps = []
    metas = []
    for core in range(8):
        b, rank = core // RANKS, core % RANKS
        pos = [snake_pos(c, rank, s) for s in range(SL)]
        rows = np.concatenate([np.arange(p * 128, (p + 1) * 128) for p in pos])
        x_own = np.ascontiguousarray(x[b][rows])
        cosq, snegq, sposq = rope_tables(cfg, rows)
        masks = np.zeros((SL * 4, 128, 512), dtype=bfdt)
        tloc = np.arange(128)[:, None]
        qloc = np.arange(128)[None, :]
        for s in range(SL):
            for dt in range(4):
                tt = c["T"][s] - 4 + dt
                m = ((tt * 128 + tloc) <= (pos[s] * 128 + qloc)).astype(np.float32)
                masks[s * 4 + dt] = np.tile(m, (1, 4)).astype(bfdt)
        im = dict(shared)
        im.update(x_own=x_own, x_full=np.ascontiguousarray(x[b]),
                  cosq=np.ascontiguousarray(cosq),
                  snegq=np.ascontiguousarray(snegq),
                  sposq=np.ascontiguousarray(sposq), masks=masks)
        in_maps.append(im)
        metas.append((b, rows))
    return in_maps, metas


def assemble_output(cfg, results, metas):
    c = derive(cfg)
    out = np.empty((cfg["B"], c["S"], c["D"]), np.float32)
    for core, ((b, rows), r) in enumerate(zip(metas, results)):
        out[b][rows] = np.asarray(r["out_own"], np.float32)
    return out


def kernel(L0_attn_x, L0_attn_w_qkv, L0_attn_w_o, L0_ffn_w_gate, L0_ffn_w_up,
           L0_ffn_w_down):
    cfg = full_cfg()
    inputs = dict(L0_attn_x=L0_attn_x, L0_attn_w_qkv=L0_attn_w_qkv,
                  L0_attn_w_o=L0_attn_w_o, L0_ffn_w_gate=L0_ffn_w_gate,
                  L0_ffn_w_up=L0_ffn_w_up, L0_ffn_w_down=L0_ffn_w_down)
    in_maps, metas = make_host_inputs(inputs, cfg)
    nc = build_program(cfg)
    res = run_bass_kernel_spmd(nc, in_maps, core_ids=list(range(8)))
    return assemble_output(cfg, res.results, metas)


# version 15
# speedup vs baseline: 1.1472x; 1.0279x over previous
import sys
if "/opt/trn_rl_repo" not in sys.path:
    sys.path.insert(0, "/opt/trn_rl_repo")
import math
import contextlib
import numpy as np
import ml_dtypes
import concourse.bass as bass
import concourse.mybir as mybir
from concourse.bass_utils import run_bass_kernel_spmd

F32 = mybir.dt.float32
BF16 = mybir.dt.bfloat16
AF = mybir.ActivationFunctionType
ALU = mybir.AluOpType
bfdt = ml_dtypes.bfloat16
f32 = np.float32

ENGINES = ("sync", "gpsimd", "tensor", "vector", "scalar")
RANKS = 4


def full_cfg():
    return dict(B=2, S=4096, D=4096, HD=128, NQH=32, NKVH=8, I=11008,
                theta=10000.0, eps=1e-5)


def derive(cfg):
    c = dict(cfg)
    c["HT"] = c["D"] // 128
    c["ST"] = c["S"] // 128
    c["SL"] = c["ST"] // RANKS
    c["OWN"] = c["SL"] * 128
    c["RB"] = min(512, c["OWN"])
    c["NRB"] = c["OWN"] // c["RB"]
    c["IT"] = c["I"] // 128
    nsb = 4
    base = c["IT"] // nsb
    rem = c["IT"] % nsb
    c["SBS"] = [base + (1 if i < rem else 0) for i in range(nsb)]
    c["T"] = [c["ST"] - RANKS * s for s in range(c["SL"])]
    c["GP"] = 4
    assert c["NQH"] == c["NKVH"] * c["GP"]
    assert c["HD"] == 128
    return c


def snake_pos(cfg, rank, s):
    base = cfg["ST"] - 1 - RANKS * s
    return base - rank if s % 2 == 0 else base - (RANKS - 1) + rank


class KB:
    def __init__(self):
        self.q = {e: [] for e in ENGINES}
        self.cnt = {}
        self.waited = {e: {} for e in ENGINES}
        self.last_real = {e: None for e in ENGINES}
        self.dma_sems = {e: set() for e in ENGINES}
        self.dma_since_bar = {e: False for e in ENGINES}
        self.sem_names = []

    def sem(self, name):
        if name not in self.cnt:
            self.cnt[name] = 0
            self.sem_names.append(name)
        return name

    def val(self, sem):
        return self.cnt.get(sem, 0)

    def op(self, eng, fn, inc=None):
        rec = {"fn": fn, "incs": []}
        if inc is not None:
            self.sem(inc)
            self.cnt[inc] += 1
            rec["incs"].append((inc, 1))
            rec["val"] = (inc, self.cnt[inc])
        self.q[eng].append(rec)
        self.last_real[eng] = rec
        return self.cnt[inc] if inc is not None else None

    def dma(self, eng, fn, sem):
        self.sem(sem)
        self.cnt[sem] += 16
        rec = {"fn": fn, "incs": [(sem, 16)]}
        self.q[eng].append(rec)
        self.dma_sems[eng].add(sem)
        self.dma_since_bar[eng] = True
        return self.cnt[sem]

    def wait(self, eng, sem, val):
        if sem is None or val is None or val <= 0:
            return
        w = self.waited[eng]
        if w.get(sem, 0) >= val:
            return
        w[sem] = val
        self.q[eng].append({"wait": (sem, val)})

    def barrier(self):
        ph = self.sem("ph")
        for e in ENGINES:
            if self.dma_since_bar[e]:
                for dsem in sorted(self.dma_sems[e]):
                    self.wait(e, dsem, self.cnt[dsem])
                self.cnt[ph] += 1
                self.q[e].append({"nopinc": (ph, 1)})
                self.dma_since_bar[e] = False
            if self.last_real[e] is not None:
                rec = self.last_real[e]
                if rec["incs"]:
                    s, v = rec["val"]
                    self.wait(e, s, v)
                    self.cnt[ph] += 1
                    self.q[e].append({"nopinc": (ph, 1)})
                else:
                    self.cnt[ph] += 1
                    rec["incs"].append((ph, 1))
                    rec["val"] = (ph, self.cnt[ph])
                self.last_real[e] = None
        tgt = self.cnt[ph]
        for e in ENGINES:
            self.wait(e, ph, tgt)

    def runner(self, eng, semobjs):
        lst = self.q[eng]

        def run(e):
            for rec in lst:
                if "wait" in rec:
                    s, v = rec["wait"]
                    e.wait_ge(semobjs[s], v)
                elif "nopinc" in rec:
                    s, n = rec["nopinc"]
                    e.nop().then_inc(semobjs[s], n)
                else:
                    ins = rec["fn"](e)
                    for s, n in rec["incs"]:
                        ins = ins.then_inc(semobjs[s], n)
        return run


def build_program(cfg, phases=99):
    c = derive(cfg)
    D, S, HT, ST, SL = c["D"], c["S"], c["HT"], c["ST"], c["SL"]
    OWN, RB, NRB = c["OWN"], c["RB"], c["NRB"]
    NQH, NKVH, GP, IT = c["NQH"], c["NKVH"], c["GP"], c["IT"]
    HD = 128
    NCH = 512
    nc = bass.Bass("TRN2", target_bir_lowering=False, debug=False)

    ap = lambda t: t.ap()
    x_own = ap(nc.dram_tensor("x_own", [OWN, D], F32, kind="ExternalInput"))
    x_full = ap(nc.dram_tensor("x_full", [S, D], F32, kind="ExternalInput"))
    NPASS = (NKVH + 1) // 2
    PW = min(2, NKVH) * 256
    w_q = ap(nc.dram_tensor("w_q", [NQH, 128, HT, 128], F32, kind="ExternalInput"))
    w_kv = ap(nc.dram_tensor("w_kv", [NPASS, 128, HT, PW], F32, kind="ExternalInput"))
    w_o = ap(nc.dram_tensor("w_o", [HT, 128, HT, 128], F32, kind="ExternalInput"))
    w_guT = ap(nc.dram_tensor("w_guT", [IT, 128, 2 * HT * 128], F32, kind="ExternalInput"))
    w_dT = ap(nc.dram_tensor("w_dT", [HT, 128, IT, 128], F32, kind="ExternalInput"))
    identb_d = ap(nc.dram_tensor("identb", [128, 128], BF16, kind="ExternalInput"))
    identf_d = ap(nc.dram_tensor("identf", [128, 128], F32, kind="ExternalInput"))
    onesc_d = ap(nc.dram_tensor("onesc", [128, 1], F32, kind="ExternalInput"))
    onesr_d = ap(nc.dram_tensor("onesr", [1, 128], F32, kind="ExternalInput"))
    onesb_d = ap(nc.dram_tensor("onesb", [128, 1], BF16, kind="ExternalInput"))
    cosd_d = ap(nc.dram_tensor("cosd", [128, S], F32, kind="ExternalInput"))
    sneg_d = ap(nc.dram_tensor("sneg", [64, S], F32, kind="ExternalInput"))
    spos_d = ap(nc.dram_tensor("spos", [64, S], F32, kind="ExternalInput"))
    cosq_d = ap(nc.dram_tensor("cosq", [128, OWN], F32, kind="ExternalInput"))
    snegq_d = ap(nc.dram_tensor("snegq", [64, OWN], F32, kind="ExternalInput"))
    sposq_d = ap(nc.dram_tensor("sposq", [64, OWN], F32, kind="ExternalInput"))
    masks_d = ap(nc.dram_tensor("masks", [SL * 4, 128, 512], BF16, kind="ExternalInput"))
    out_d = ap(nc.dram_tensor("out_own", [OWN, D], F32, kind="ExternalOutput"))

    xnT_own_d = ap(nc.dram_tensor("xnT_own_d", [HT, 128, OWN], BF16, kind="Internal"))
    xnT_full_d = ap(nc.dram_tensor("xnT_full_d", [HT, 128, S], BF16, kind="Internal"))
    q_d = ap(nc.dram_tensor("q_d", [NKVH, SL, GP, 128, 128], BF16, kind="Internal"))
    attnT_d = ap(nc.dram_tensor("attnT_d", [HT, 128, OWN], BF16, kind="Internal"))
    y_d = ap(nc.dram_tensor("y_d", [HT, 128, OWN], F32, kind="Internal"))


    kb = KB()
    es = contextlib.ExitStack()

    def sb(name, shape, dt):
        return es.enter_context(nc.sbuf_tensor(name, shape, dt))

    def ps(name, shape, dt):
        return es.enter_context(nc.psum_tensor(name, shape, dt))

    AR = sb("AR", [128, 65536], BF16)
    ARF = AR[:, :].bitcast(F32)
    U1 = sb("U1", [128, 16384], BF16)
    U1F = U1[:, :].bitcast(F32)
    U2 = sb("U2", [128, 8192], BF16)
    U2F = U2[:, :].bitcast(F32)
    U3 = sb("U3", [128, 6144], BF16)
    U3F = U3[:, :].bitcast(F32)
    identb = sb("identb_s", [128, 128], BF16)
    identf = sb("identf_s", [128, 128], F32)
    onesc = sb("onesc_s", [128, 1], F32)
    onesr = sb("onesr_s", [1, 128], F32)
    stat6 = [sb(f"stat6{i}", [128, 6 * (D // NCH)], F32) for i in range(2)]
    stat2 = [sb(f"stat2{i}", [128, 2], F32) for i in range(2)]
    rstd = [sb(f"rstd{i}", [128, 1], F32) for i in range(2)]
    negmr = [sb(f"negmr{i}", [128, 1], F32) for i in range(2)]
    recip = sb("recip", [1, 1024], F32)
    bcs = sb("bcs", [128, 512], F32)

    PM = ps("PM", [128, 2048], F32)
    pm = [PM[:, i * 512:(i + 1) * 512] for i in range(4)]
    ptr = ps("ptr", [128, 2048], BF16)
    ptrF = [ptr[:, 0:1024].bitcast(F32), ptr[:, 1024:2048].bitcast(F32)]
    pl = ps("pl", [128, 512], F32)
    pbc = ps("pbc", [128, 512], F32)

    SC = np.float32(1.0 / math.sqrt(HD))
    EB = np.float32(-16.0)

    for dst, src in ((identb, identb_d), (identf, identf_d), (onesc, onesc_d),
                     (onesr, onesr_d)):
        kb.dma("sync", (lambda d=dst, s=src: lambda e: e.dma_start(out=d[:, :], in_=s))(), "dC")
    vC = kb.val("dC")

    AEND = 32768
    xbuf = [U1F[:, 0:D], U1F[:, D:2 * D],
            ARF[:, AEND - 2 * D:AEND - D], ARF[:, AEND - D:AEND]]
    NXB = 4
    xnbuf = [U2[:, 0:D], U2[:, D:2 * D]]
    xnT_own = AR[:, 0:HT * OWN].rearrange("p (h r) -> p h r", h=HT)
    PMB = PM[:, :].bitcast(BF16)
    LN_SLOTS = [(ptr, 0), (ptr, 1024), (PMB, 0), (PMB, 1024), (PMB, 2048), (PMB, 3072)]
    kb._ptr_evac = {i: (("sES" if i % 2 == 0 else "vT"), 0) for i in range(6)}
    kb._ln_slot_rr = [0]

    def ln_pass(ntiles, src_of, dst_of, store_of, tag):
        nch = D // NCH
        state = {}
        xg = kb._lnp_xg if hasattr(kb, "_lnp_xg") else {i: ("sS", 0) for i in range(NXB)}
        tpg = kb._lnp_tpg if hasattr(kb, "_lnp_tpg") else {0: 0, 1: 0}
        for t in range(ntiles + 1):
            if t >= 1:
                u = t - 1
                par = u % 2
                xpar = u % NXB
                kb.wait("vector", "vS", state.pop(u))
                vagg = kb.op("vector",
                             (lambda par=par: lambda e: e.bn_aggr(stat2[par][:, :],
                                                                  stat6[par][:, :]))(),
                             inc="vS")
                kb.wait("scalar", "vS", vagg)
                s1 = kb.op("scalar",
                           (lambda par=par: lambda e: e.activation(
                               rstd[par][:, :], stat2[par][:, 1:2], AF.Ln,
                               bias=np.float32(c["eps"])))(), inc="sS")
                kb.wait("scalar", "sS", s1)
                s2 = kb.op("scalar",
                           (lambda par=par: lambda e: e.activation(
                               rstd[par][:, :], rstd[par][:, :], AF.Exp,
                               scale=np.float32(-0.5)))(), inc="sS")
                kb.wait("vector", "sS", s2)
                v2 = kb.op("vector",
                           (lambda par=par: lambda e: e.tensor_scalar(
                               negmr[par][:, :], stat2[par][:, 0:1],
                               rstd[par][:, 0:1], -1.0, ALU.mult, ALU.mult))(),
                           inc="vS")
                kb.wait("scalar", "vS", v2)
                kb.wait("scalar", "tS", tpg[par])
                hready = {}
                HHF = max(1, HT // 2)
                for hf in range(0, HT, HHF):
                    r = kb.op("scalar",
                              (lambda par=par, xpar=xpar, hf=hf, HHF=HHF: lambda e: e.activation(
                                  xnbuf[par][:, hf * 128:(hf + HHF) * 128],
                                  xbuf[xpar][:, hf * 128:(hf + HHF) * 128], AF.Identity,
                                  bias=negmr[par][:, 0:1],
                                  scale=rstd[par][:, 0:1]))(), inc="sS")
                    for h2 in range(hf, hf + HHF):
                        hready[h2] = r
                xg[xpar] = ("sS", r)
            if t < ntiles:
                xpar2 = t % NXB
                par2 = t % 2
                s_, v_ = xg[xpar2]
                kb.wait("sync", s_, v_)
                dv = kb.dma("sync",
                            (lambda t=t, xpar2=xpar2: lambda e: e.dma_start(
                                out=xbuf[xpar2][:, :], in_=src_of(t)))(),
                            f"dX{xpar2}")
                kb.wait("vector", f"dX{xpar2}", dv)
                for ch in range(nch):
                    kb.op("vector",
                          (lambda ch=ch, par2=par2, xpar2=xpar2: lambda e: e.bn_stats(
                              stat6[par2][:, ch * 6:(ch + 1) * 6],
                              xbuf[xpar2][:, ch * NCH:(ch + 1) * NCH]))(),
                          inc="vS" if ch == nch - 1 else None)
                state[t] = kb.val("vS")
            if t >= 1:
                GW = min(8, HT)
                dfn = dst_of(u)
                evs = {}
                for ht0 in range(0, HT, GW):
                    slot = kb._ln_slot_rr[0] % 6
                    kb._ln_slot_rr[0] += 1
                    sbase, soff = LN_SLOTS[slot]
                    esem, eval_ = kb._ptr_evac[slot]
                    kb.wait("tensor", esem, eval_)
                    kb.wait("tensor", "dC", vC)
                    for k in range(GW):
                        ht = ht0 + k
                        kb.wait("tensor", "sS", hready[ht])
                        kb.op("tensor",
                              (lambda ht=ht, k=k, par=par, sbase=sbase, soff=soff: lambda e: e.transpose(
                                  sbase[:, soff + k * 128: soff + (k + 1) * 128],
                                  xnbuf[par][:, ht * 128:(ht + 1) * 128], identb[:, :]))(),
                              inc="tS" if k == GW - 1 else None)
                    tv = kb.val("tS")
                    eng, sem = ("scalar", "sES") if slot % 2 == 0 else ("vector", "vT")
                    kb.wait(eng, "tS", tv)
                    ev = kb.op(eng,
                               (lambda dfn=dfn, ht0=ht0, eng=eng, GW=GW, sbase=sbase, soff=soff: lambda e: (
                                   e.copy if eng == "scalar" else e.tensor_copy)(
                                   dfn(ht0),
                                   sbase[:, soff: soff + GW * 128].rearrange(
                                       "p (a b) -> p a b", a=GW)))(),
                               inc=sem)
                    kb._ptr_evac[slot] = (sem, ev)
                    evs[sem] = ev
                tpg[par] = kb.val("tS")
                store_of(u, evs)
        kb._lnp_xg = xg
        kb._lnp_tpg = tpg

    def pa_store(u, evs):
        for sem, v in evs.items():
            kb.wait("gpsimd", sem, v)
        kb.dma("gpsimd",
               (lambda u=u: lambda e: e.dma_start(
                   out=xnT_own_d.rearrange("h p r -> p h r")[:, :, u * 128:(u + 1) * 128],
                   in_=xnT_own[:, :, u * 128:(u + 1) * 128]))(),
               "dXO")

    ln_pass(OWN // 128,
            lambda t: x_own[t * 128:(t + 1) * 128, :],
            lambda u: (lambda ht0: xnT_own[:, ht0:ht0 + min(8, HT), u * 128:(u + 1) * 128]),
            pa_store, "pa")

    nst = 4
    stg3 = [AR[:, 32768 + i * HT * 128: 32768 + (i + 1) * HT * 128]
            .rearrange("p (h r) -> p h r", h=HT) for i in range(nst)]
    kb._pb_st = {i: 0 for i in range(nst)}

    def pb_dst(u):
        st_ = u % nst
        kb.wait("scalar", f"dXF{st_}", kb._pb_st[st_])
        kb.wait("vector", f"dXF{st_}", kb._pb_st[st_])
        return lambda ht0: stg3[st_][:, ht0:ht0 + min(8, HT), :]

    def pb_store(u, evs):
        st_ = u % nst
        for sem, v in evs.items():
            kb.wait("gpsimd", sem, v)
        kb._pb_st[st_] = kb.dma("gpsimd",
               (lambda u=u, st_=st_: lambda e: e.dma_start(
                   out=xnT_full_d.rearrange("h p t -> p h t")[:, :, u * 128:(u + 1) * 128],
                   in_=stg3[st_][:, :, :]))(),
               f"dXF{st_}")

    ln_pass(ST,
            lambda t: x_full[t * 128:(t + 1) * 128, :],
            pb_dst, pb_store, "pb")
    kb.barrier()
    if phases <= 1:
        return _finish(nc, kb, es)

    wq = [U1[:, i * (HT * 128):(i + 1) * (HT * 128)] for i in range(2)]
    cosq = U2F[:, 0:OWN]
    snegq = U2F[0:64, OWN:OWN + OWN]
    sposq = U2F[0:64, 2 * OWN:3 * OWN]
    t1b = U3F[:, 0:512]
    t2b = U3F[:, 512:1024]
    qpiece = U3[:, 2048:2048 + OWN]

    kb.dma("sync", lambda e: e.dma_start(out=cosq[:, :], in_=cosq_d), "dT")
    kb.dma("sync", lambda e: e.dma_start(out=snegq[:, :], in_=snegq_d), "dT")
    kb.dma("sync", lambda e: e.dma_start(out=sposq[:, :], in_=sposq_d), "dT")
    vT0 = kb.val("dT")

    def rope(psrc, cos_ap, sneg_ap, spos_ap, dst_ap, deps, tag):
        N = dst_ap.shape[-1]
        for s, v in deps:
            kb.wait("vector", s, v)
        kb.wait("vector", "vS", getattr(kb, "_rope_war", 0))
        kb.op("vector", lambda e: e.tensor_tensor(t1b[:, 0:N], psrc[:, 0:N], cos_ap, ALU.mult), inc="vS")
        kb.op("vector", lambda e: e.tensor_tensor(t2b[0:64, 0:N], psrc[64:128, 0:N], sneg_ap, ALU.mult), inc="vS")
        kb.op("vector", lambda e: e.tensor_tensor(t2b[64:128, 0:N], psrc[0:64, 0:N], spos_ap, ALU.mult), inc="vS")
        v4 = kb.val("vS")
        kb.wait("vector", "vS", v4)
        r = kb.op("vector", lambda e: e.tensor_tensor(dst_ap, t1b[:, 0:N], t2b[:, 0:N], ALU.add), inc="vS")
        kb._rope_war = r
        return r

    qs_done = 0
    for h in range(NQH):
        par = h % 2
        if h >= 2:
            kb.wait("gpsimd", "tS", kb._wq_used[par])
        if not hasattr(kb, "_wq_used"):
            kb._wq_used = {}
        dv = kb.dma("gpsimd",
                    (lambda h=h, par=par: lambda e: e.dma_start(
                        out=wq[par].rearrange("p (h c) -> p h c", h=HT),
                        in_=w_q[h]))(),
                    f"dWQ{par}")
        rope_done = None
        for rb in range(NRB):
            bank = pm[rb % 2]
            kb.wait("tensor", f"dWQ{par}", dv)
            kb.wait("tensor", "ph", kb.val("ph"))
            if hasattr(kb, "_pcq_bank") and (rb % 2) in kb._pcq_bank:
                kb.wait("tensor", "vS", kb._pcq_bank[rb % 2])
            for ht in range(HT):
                mm = kb.op("tensor",
                           (lambda ht=ht, rb=rb, bank=bank, par=par: lambda e: e.matmul(
                               bank[:, 0:RB], wq[par][:, ht * 128:(ht + 1) * 128],
                               xnT_own[:, ht, rb * RB:(rb + 1) * RB],
                               start=(ht == 0), stop=(ht == HT - 1)))(),
                           inc="tS" if ht == HT - 1 else None)
            if not hasattr(kb, "_wq_used"):
                kb._wq_used = {}
            kb._wq_used[par] = kb.val("tS")
            deps = [("tS", kb.val("tS"))]
            if rb == 0 and qs_done:
                deps.append(("dQ", qs_done))
            rope_done = rope(bank,
                             cosq[:, rb * RB:(rb + 1) * RB],
                             snegq[0:64, rb * RB:(rb + 1) * RB],
                             sposq[0:64, rb * RB:(rb + 1) * RB],
                             qpiece[:, rb * RB:(rb + 1) * RB],
                             deps + [("dT", vT0)], f"q{h}")
            if not hasattr(kb, "_pcq_bank"):
                kb._pcq_bank = {}
            kb._pcq_bank[rb % 2] = kb.val("vS")
        g, hh = h // GP, h % GP
        kb.wait("sync", "vS", rope_done)
        qs_done = kb.dma("sync",
                         (lambda g=g, hh=hh: lambda e: e.dma_start(
                             out=q_d[g, :, hh, :, :].rearrange("s p q -> p s q"),
                             in_=qpiece[:, :].rearrange("p (s q) -> p s q", q=128)))(),
                         "dQ")
    kb.barrier()
    if phases <= 2:
        return _finish(nc, kb, es)

    Kc = AR[:, 0:NKVH * S]
    Vc = AR[:, 32768:32768 + NKVH * S]
    wkv = U1[:, 0:HT * 512]
    NCK = 4
    xnf = [U2[:, i * 2048:(i + 1) * 2048] for i in range(NCK)]
    ctb = [U3F[:, 1024:1536], U3F[:, 1024:1536]]
    sntb = [U3F[0:64, 1536:2048], U3F[0:64, 1536:2048]]
    sptb = [U3F[0:64, 2048:2560], U3F[0:64, 2048:2560]]
    vtmp = U3[:, 5120:5632]

    kb._pd_bankg = {u: ("tS", 0) for u in range(4)}
    plB = pl[:, :].bitcast(BF16)
    pbcB = pbc[:, :].bitcast(BF16)
    PD_SLOTS = [(ptr, 0), (ptr, 1024), (plB, 0), (pbcB, 0)]
    kb._ptr_evac = {0: 0, 1: 0, 2: 0, 3: 0}
    kb._pd_vrr = [0]
    for pp in range(NPASS):
        g0 = 2 * pp
        ng = min(2, NKVH - g0)
        kb.wait("gpsimd", "tS", kb.val("tS"))
        HQ = max(1, HT // 4)
        dwv = {}
        for qtr in range((HT + HQ - 1) // HQ):
            h0, h1 = qtr * HQ, min(HT, (qtr + 1) * HQ)
            dwv[qtr] = kb.dma("gpsimd",
                        (lambda pp=pp, ng=ng, h0=h0, h1=h1: lambda e: e.dma_start(
                            out=wkv.rearrange("p (h c) -> p h c", h=HT)[:, h0:h1, 0:ng * 256],
                            in_=w_kv[pp][:, h0:h1, 0:ng * 256]))(),
                        f"dWKV{qtr}")
        for tb in range(S // 512):
            tpar = tb % 2
            kb.wait("scalar", "vS", kb._pd_trig_g if hasattr(kb, "_pd_trig_g") else 0)
            for dst, src in ((ctb[tpar], cosd_d[:, tb * 512:(tb + 1) * 512]),
                             (sntb[tpar], sneg_d[:, tb * 512:(tb + 1) * 512]),
                             (sptb[tpar], spos_d[:, tb * 512:(tb + 1) * 512])):
                kb.dma("scalar", (lambda d=dst, s=src: lambda e: e.dma_start(out=d, in_=s))(), "dTT")
            vtrig = kb.val("dTT")
            nck = (HT + 3) // 4
            for ck in range(nck):
                cpar = (tb * nck + ck) % NCK
                nht = min(4, HT - ck * 4)
                if hasattr(kb, "_xnf_used") and cpar in kb._xnf_used:
                    kb.wait("sync", "tS", kb._xnf_used[cpar])
                dv = kb.dma("sync",
                            (lambda ck=ck, cpar=cpar, tb=tb, nht=nht: lambda e: e.dma_start(
                                out=xnf[cpar].rearrange("p (a t) -> p a t", a=4)[:, 0:nht, :],
                                in_=xnT_full_d.rearrange("h p t -> p h t")[:, ck * 4:ck * 4 + nht,
                                                                           tb * 512:(tb + 1) * 512]))(),
                            f"dXN{cpar}")
                kb.wait("tensor", f"dXN{cpar}", dv)
                if ck == 0:
                    for u, (semn, valn) in kb._pd_bankg.items():
                        kb.wait("tensor", semn, valn)
                for hl in range(nht):
                    ht = ck * 4 + hl
                    kb.wait("tensor", f"dWKV{ht // HQ}", dwv[ht // HQ])
                    for u in range(2 * ng):
                        mm = kb.op("tensor",
                                   (lambda u=u, ht=ht, hl=hl, cpar=cpar: lambda e: e.matmul(
                                       pm[u][:, :], wkv[:, ht * 512 + u * 128: ht * 512 + (u + 1) * 128],
                                       xnf[cpar][:, hl * 512:(hl + 1) * 512],
                                       start=(ht == 0), stop=(ht == HT - 1)))(),
                                   inc="tS" if (hl == nht - 1 and u == 2 * ng - 1) else None)
                if not hasattr(kb, "_xnf_used"):
                    kb._xnf_used = {}
                kb._xnf_used[cpar] = kb.val("tS") + (0 if ck < nck - 1 else 0)
            tmm = kb.val("tS")
            for gl in range(ng):
                g = g0 + gl
                rope_done = rope(pm[2 * gl],
                                 ctb[tpar][:, :], sntb[tpar][0:64, :], sptb[tpar][0:64, :],
                                 Kc[:, g * S + tb * 512: g * S + (tb + 1) * 512],
                                 [("tS", tmm), ("dTT", vtrig)], f"k{g}t{tb}")
                kb._pd_bankg[2 * gl] = ("vS", rope_done)
                kb._pd_trig_g = rope_done
                kb.wait("scalar", "tS", tmm)
                if hasattr(kb, "_vtmp_used"):
                    kb.wait("scalar", "tS", kb._vtmp_used)
                sc = kb.op("scalar",
                           (lambda gl=gl: lambda e: e.copy(vtmp[:, :], pm[2 * gl + 1][:, :]))(),
                           inc="sS")
                kb._pd_bankg[2 * gl + 1] = ("sS", sc)
                slot = kb._pd_vrr[0] % 4
                kb._pd_vrr[0] += 1
                sb_t, so_t = PD_SLOTS[slot]
                kb.wait("tensor", "sS", sc)
                kb.wait("tensor", "vT", kb._ptr_evac[slot])
                for i in range(4):
                    kb.op("tensor",
                          (lambda i=i, sb_t=sb_t, so_t=so_t: lambda e: e.transpose(
                              sb_t[:, so_t + i * 128: so_t + (i + 1) * 128],
                              vtmp[:, i * 128:(i + 1) * 128], identb[:, :]))(),
                          inc="tS" if i == 3 else None)
                kb._vtmp_used = kb.val("tS")
                kb.wait("vector", "tS", kb.val("tS"))
                ev = kb.op("vector",
                           (lambda g=g, tb=tb, sb_t=sb_t, so_t=so_t: lambda e: e.tensor_copy(
                               Vc[:, g * S + tb * 512: g * S + (tb + 1) * 512],
                               sb_t[:, so_t: so_t + 512]))(),
                           inc="vT")
                kb._ptr_evac[slot] = ev
    kb.barrier()
    if phases <= 3:
        return _finish(nc, kb, es)

    QGW = SL * GP * 128
    qg = [U2[:, i * QGW:(i + 1) * QGW] for i in range(2)]
    maskbuf = U1[:, 0:SL * 4 * 512]
    pbuf = [U3[:, i * 1024:(i + 1) * 1024] for i in range(4)]
    stage = [U3[:, 4096 + i * 512:4096 + (i + 1) * 512] for i in range(2)]
    onesb = U3[:, 5120:5136]
    dm = kb.dma("sync", lambda e: e.dma_start(
        out=maskbuf[:, :].rearrange("p (m q) -> p m q", q=512),
        in_=masks_d.rearrange("m p q -> p m q")), "dM")
    kb.dma("sync", lambda e: e.dma_start(out=onesb[:, 0:1], in_=onesb_d), "dM")
    dm = kb.val("dM")

    kb._pe_pv_guard = {0: 0, 1: 0}
    kb._pe_pl_guard = {0: 0, 1: 0}
    kb._pe_pbc_guard = 0
    kb._pe_sc_guard = {0: 0, 1: 0}
    kb._pe_p_guard = {i: (0, 0) for i in range(4)}
    kb._pe_stage_guard = {0: 0, 1: 0}
    kb._pe_tails = []
    qload = {}

    def emit_scores(lane, g, s, par, bb, Tn):
        pb_ = 2 * lane + (bb % 2)
        tts = (2 * bb, 2 * bb + 1)
        kb.wait("tensor", qload[g][0], qload[g][1])
        kb.wait("tensor", "sS", kb._pe_sc_guard[lane])
        if bb == 0:
            kb.wait("tensor", "dM", dm)
            kb.wait("tensor", "vS", kb._pe_pv_guard[lane])
            kb.wait("tensor", "vS", kb._pe_pl_guard[lane])
        for k, tt in enumerate(tts):
            smm = kb.op("tensor",
                        (lambda g=g, s=s, tt=tt, lane=lane, k=k, par=par: lambda e: e.matmul(
                            PM[:, lane * 1024 + k * 512: lane * 1024 + (k + 1) * 512],
                            Kc[:, g * S + tt * 128:g * S + (tt + 1) * 128],
                            qg[par][:, s * GP * 128:(s + 1) * GP * 128],
                            start=True, stop=True))(),
                        inc="tS" if k == 1 else None)
        kb.wait("scalar", "tS", smm)
        kb.wait("scalar", "tS", kb._pe_p_guard[pb_][0])
        kb.wait("scalar", "vS", kb._pe_p_guard[pb_][1])
        ex = kb.op("scalar",
                   (lambda lane=lane, pb_=pb_: lambda e: e.activation(
                       pbuf[pb_][:, :], PM[:, lane * 1024:(lane + 1) * 1024],
                       AF.Exp, bias=EB, scale=SC))(),
                   inc="sS")
        kb._pe_sc_guard[lane] = ex
        mg = 0
        for k, tt in enumerate(tts):
            dt = tt - (Tn - 4)
            if dt >= 0:
                kb.wait("vector", "sS", ex)
                kb.wait("vector", "dM", dm)
                kb.op("vector",
                      (lambda s=s, dt=dt, pb_=pb_, k=k: lambda e: e.tensor_tensor(
                          pbuf[pb_][:, k * 512:(k + 1) * 512],
                          pbuf[pb_][:, k * 512:(k + 1) * 512],
                          maskbuf[:, (s * 4 + dt) * 512:(s * 4 + dt + 1) * 512],
                          ALU.mult))(),
                      inc="vS")
                mg = kb.val("vS")
        return (tts, pb_, mg, ex)

    def emit_pvl(lane, g, Tn, pend):
        tts, pb_, mg, ex = pend
        if mg:
            kb.wait("tensor", "vS", mg)
        else:
            kb.wait("tensor", "sS", ex)
        lm = 0
        for k, tt in enumerate(tts):
            kb.op("tensor",
                  (lambda g=g, tt=tt, pb_=pb_, k=k, Tn=Tn, lane=lane: lambda e: e.matmul(
                      ptrF[lane][:, :], Vc[:, g * S + tt * 128:g * S + (tt + 1) * 128],
                      pbuf[pb_][:, k * 512:(k + 1) * 512],
                      start=(tt == 0), stop=(tt == Tn - 1)))(),
                  inc=None)
            lm = kb.op("tensor",
                       (lambda tt=tt, pb_=pb_, k=k, Tn=Tn, lane=lane: lambda e: e.matmul(
                           pl[64 * lane:64 * lane + 1, :], onesb[:, 0:1],
                           pbuf[pb_][:, k * 512:(k + 1) * 512],
                           start=(tt == 0), stop=(tt == Tn - 1)))(),
                       inc="tS")
        kb._pe_p_guard[pb_] = (lm, mg)

    def emit_tail(lane, g, s):
        pv_done = kb.val("tS")
        kb.wait("vector", "tS", pv_done)
        rc = kb.op("vector",
                   (lambda lane=lane: lambda e: e.reciprocal(
                       recip[0:1, lane * 512:(lane + 1) * 512],
                       pl[64 * lane:64 * lane + 1, :]))(), inc="vS")
        kb._pe_pl_guard[lane] = rc
        kb.wait("tensor", "vS", rc)
        kb.wait("tensor", "sS", kb._pe_pbc_guard)
        bc = kb.op("tensor",
                   (lambda lane=lane: lambda e: e.matmul(
                       pbc[:, :], onesr[:, :], recip[0:1, lane * 512:(lane + 1) * 512],
                       start=True, stop=True))(), inc="tS")
        kb.wait("scalar", "tS", bc)
        cp = kb.op("scalar", lambda e: e.copy(bcs[:, :], pbc[:, :]), inc="sS")
        kb._pe_pbc_guard = cp
        spar = (g * SL + s) % 2
        kb.wait("vector", "sS", cp)
        kb.wait("vector", f"dA{spar}", kb._pe_stage_guard[spar])
        dv = kb.op("vector",
                   (lambda spar=spar, lane=lane: lambda e: e.tensor_tensor(
                       stage[spar][:, :], ptrF[lane][:, :], bcs[:, :], ALU.mult))(),
                   inc="vS")
        kb._pe_pv_guard[lane] = dv
        kb.wait("sync", "vS", dv)
        stv = kb.dma("sync",
                     (lambda g=g, s=s, spar=spar: lambda e: e.dma_start(
                         out=attnT_d.rearrange("h p r -> p h r")[:, g * GP:(g + 1) * GP,
                                                                 s * 128:(s + 1) * 128],
                         in_=stage[spar][:, :].rearrange("p (h q) -> p h q", h=GP)))(),
                     f"dA{spar}")
        kb._pe_stage_guard[spar] = stv

    for g in range(NKVH):
        par = g % 2
        if g not in qload:
            if g >= 2:
                kb.wait("sync", "tS", kb._qg_used[par])
            qload[g] = (f"dQG{par}", kb.dma("sync",
                              (lambda g=g, par=par: lambda e: e.dma_start(
                                  out=qg[par][:, :].rearrange("p (s h q) -> p s h q", s=SL, h=GP),
                                  in_=q_d[g].rearrange("s h p q -> p s h q")))(),
                              f"dQG{par}"))
        if g + 1 < NKVH:
            par2 = (g + 1) % 2
            if g + 1 >= 2:
                kb.wait("sync", "tS", kb._qg_used.get(par2, 0) if hasattr(kb, "_qg_used") else 0)
            qload[g + 1] = (f"dQG{par2}", kb.dma("sync",
                                  (lambda g=g + 1, par=par2: lambda e: e.dma_start(
                                      out=qg[par][:, :].rearrange("p (s h q) -> p s h q", s=SL, h=GP),
                                      in_=q_d[g].rearrange("s h p q -> p s h q")))(),
                                  f"dQG{par2}"))
        for j in range(SL // 2):
            sA, sB = 2 * j, 2 * j + 1
            TA, TB = c["T"][sA], c["T"][sB]
            pend = {0: None, 1: None}
            for bb in range(TA // 2):
                pend_new_a = emit_scores(0, g, sA, par, bb, TA)
                if pend[1] is not None:
                    emit_pvl(1, g, TB, pend[1])
                    pend[1] = None
                if bb < TB // 2:
                    pend[1] = emit_scores(1, g, sB, par, bb, TB)
                if pend[0] is not None:
                    emit_pvl(0, g, TA, pend[0])
                pend[0] = pend_new_a
            if pend[1] is not None:
                emit_pvl(1, g, TB, pend[1])
            if pend[0] is not None:
                emit_pvl(0, g, TA, pend[0])
            emit_tail(0, g, sA)
            emit_tail(1, g, sB)
        if not hasattr(kb, "_qg_used"):
            kb._qg_used = {}
        kb._qg_used[par] = kb.val("tS")
    kb.barrier()
    if phases <= 4:
        return _finish(nc, kb, es)

    attnT = AR[:, 0:HT * OWN].rearrange("p (h r) -> p h r", h=HT)
    wo = [U1[:, i * (HT * 128):(i + 1) * (HT * 128)] for i in range(2)]
    ybuf = [U2F[:, i * 512:(i + 1) * 512] for i in range(2)]
    kb.dma("sync", lambda e: e.dma_start(
        out=attnT[:, :, :], in_=attnT_d.rearrange("h p r -> p h r")), "dAT")
    vat = kb.val("dAT")
    kb._pf_w_used = {}
    kb._pf_bank = {}
    kb._pf_ybuf = {0: 0, 1: 0}
    for of in range(HT):
        par = of % 2
        if of >= 2:
            kb.wait("gpsimd", "tS", kb._pf_w_used[par])
        dw = kb.dma("gpsimd",
                    (lambda of=of, par=par: lambda e: e.dma_start(
                        out=wo[par].rearrange("p (h c) -> p h c", h=HT),
                        in_=w_o[of]))(),
                    f"dWO{par}")
        for rb in range(NRB):
            bank = pm[(of * NRB + rb) % 4]
            kb.wait("tensor", f"dWO{par}", dw)
            kb.wait("tensor", "dAT", vat)
            bg = kb._pf_bank.get((of * NRB + rb) % 4)
            if bg:
                kb.wait("tensor", "vS", bg)
            for ht in range(HT):
                kb.op("tensor",
                      (lambda ht=ht, rb=rb, bank=bank, par=par: lambda e: e.matmul(
                          bank[:, 0:RB], wo[par][:, ht * 128:(ht + 1) * 128],
                          attnT[:, ht, rb * RB:(rb + 1) * RB],
                          start=(ht == 0), stop=(ht == HT - 1)))(),
                      inc="tS" if ht == HT - 1 else None)
            kb._pf_w_used[par] = kb.val("tS")
            ypar = (of * NRB + rb) % 2
            kb.wait("vector", "tS", kb.val("tS"))
            kb.wait("vector", f"dY{ypar}", kb._pf_ybuf[ypar])
            ev = kb.op("vector",
                       (lambda bank=bank, ypar=ypar: lambda e: e.tensor_copy(
                           ybuf[ypar][:, 0:RB], bank[:, 0:RB]))(),
                       inc="vS")
            kb._pf_bank[(of * NRB + rb) % 4] = ev
            kb.wait("sync", "vS", ev)
            sy = kb.dma("sync",
                        (lambda of=of, rb=rb, ypar=ypar: lambda e: e.dma_start(
                            out=y_d[of, :, rb * RB:(rb + 1) * RB],
                            in_=ybuf[ypar][:, 0:RB]))(),
                        f"dY{ypar}")
            kb._pf_ybuf[ypar] = sy
    kb.barrier()
    if phases <= 5:
        return _finish(nc, kb, es)

    xnA = AR[:, 0:HT * OWN].rearrange("p (h r) -> p h r", h=HT)
    nsb2 = min(8, IT)
    base2 = IT // nsb2
    SBS2 = [base2 + (1 if i < IT % nsb2 else 0) for i in range(nsb2)]
    SBmax = max(SBS2)
    inter2 = [AR[:, HT * OWN + i * SBmax * OWN: HT * OWN + (i + 1) * SBmax * OWN]
              .rearrange("p (i r) -> p i r", r=OWN) for i in range(2)]
    wgu = [U1[:, i * (2 * HT * 128):(i + 1) * (2 * HT * 128)] for i in range(2)]
    NWD = 2
    wd = [U2[:, i * (2 * SBmax * 128):(i + 1) * (2 * SBmax * 128)] for i in range(NWD)]
    sg = [U2F[:, 3072 + i * 512:3072 + (i + 1) * 512] for i in range(2)]
    fbuf = [U3F[:, i * OWN:(i + 1) * OWN] for i in range(2)]
    kb.dma("sync", lambda e: e.dma_start(
        out=xnA[:, :, :], in_=xnT_own_d.rearrange("h p r -> p h r")), "dXA")
    vxa = kb.val("dXA")
    kb._pg_w_used = {0: 0, 1: 0}
    kb._pg_sg = {0: 0, 1: 0}
    kb._pg_fbuf = {0: 0, 1: 0}
    kb._pg_islot = {0: 0, 1: 0}
    kb._pg_fbank = {0: 0, 1: 0}
    kb._pg_wd_used = {0: 0, 1: 0, 2: 0}
    sb_starts = []
    t = 0
    for n_ in SBS2:
        sb_starts.append(t)
        t += n_

    def emit_gateup(sbi):
        it0 = sb_starts[sbi]
        sbn = SBS2[sbi]
        islot = sbi % 2
        inter_slot = inter2[islot]
        kb.wait("tensor", "tS", kb._pg_islot[islot])
        for itl in range(sbn):
            it = it0 + itl
            par = it % 2
            kb.wait("gpsimd", "tS", kb._pg_w_used[par])
            dgv = kb.dma("gpsimd",
                         (lambda it=it, par=par: lambda e: e.dma_start(
                             out=wgu[par][:, :], in_=w_guT[it]))(),
                         f"dWG{par}")
            for rb in range(NRB):
                bg_ = pm[rb % 2]
                bu_ = pm[2 + rb % 2]
                kb.wait("tensor", f"dWG{par}", dgv)
                kb.wait("tensor", "dXA", vxa)
                kb.wait("tensor", "sS", kb._pg_gate_guard.get(rb % 2, 0) if hasattr(kb, "_pg_gate_guard") else 0)
                kb.wait("tensor", "vS", kb._pg_up_guard.get(rb % 2, 0) if hasattr(kb, "_pg_up_guard") else 0)
                for ht in range(HT):
                    kb.op("tensor",
                          (lambda ht=ht, rb=rb, bg_=bg_, par=par: lambda e: e.matmul(
                              bg_[:, 0:RB], wgu[par][:, ht * 128:(ht + 1) * 128],
                              xnA[:, ht, rb * RB:(rb + 1) * RB],
                              start=(ht == 0), stop=(ht == HT - 1)))(),
                          inc="tS" if ht == HT - 1 else None)
                gmm = kb.val("tS")
                for ht in range(HT):
                    kb.op("tensor",
                          (lambda ht=ht, rb=rb, bu_=bu_, par=par: lambda e: e.matmul(
                              bu_[:, 0:RB], wgu[par][:, HT * 128 + ht * 128:HT * 128 + (ht + 1) * 128],
                              xnA[:, ht, rb * RB:(rb + 1) * RB],
                              start=(ht == 0), stop=(ht == HT - 1)))(),
                          inc="tS" if ht == HT - 1 else None)
                umm = kb.val("tS")
                kb._pg_w_used[par] = umm
                spar = rb % 2
                kb.wait("scalar", "tS", gmm)
                kb.wait("scalar", "vS", kb._pg_sg[spar])
                sl_ = kb.op("scalar",
                            (lambda bg_=bg_, spar=spar: lambda e: e.activation(
                                sg[spar][:, 0:RB], bg_[:, 0:RB], AF.Silu))(),
                            inc="sS")
                if not hasattr(kb, "_pg_gate_guard"):
                    kb._pg_gate_guard = {}
                    kb._pg_up_guard = {}
                kb._pg_gate_guard[spar] = sl_
                kb.wait("vector", "tS", umm)
                kb.wait("vector", "sS", sl_)
                iv = kb.op("vector",
                           (lambda itl=itl, rb=rb, bu_=bu_, spar=spar, inter_slot=inter_slot: lambda e: e.tensor_tensor(
                               inter_slot[:, itl, rb * RB:(rb + 1) * RB], bu_[:, 0:RB],
                               sg[spar][:, 0:RB], ALU.mult))(),
                           inc="vS")
                kb._pg_up_guard[spar] = iv
                kb._pg_sg[spar] = iv
        return kb.val("vS")

    def wd_load(sbi, ht0):
        it0 = sb_starts[sbi]
        sbn = SBS2[sbi]
        nh = min(2, HT - ht0)
        par = (ht0 // 2) % NWD
        kb.wait("gpsimd", "tS", kb._pg_wd_used[par])
        return kb.dma("gpsimd",
                     (lambda it0=it0, sbn=sbn, ht0=ht0, nh=nh, par=par: lambda e: e.dma_start(
                         out=wd[par].rearrange("p (u i c) -> p u i c", u=2, c=128)[:, 0:nh, 0:sbn, :],
                         in_=w_dT[ht0:ht0 + nh].rearrange("u p i c -> p u i c")[:, :, it0:it0 + sbn, :]))(),
                     f"dWD{par}")

    def emit_down(sbi, inter_done, pre):
        it0 = sb_starts[sbi]
        sbn = SBS2[sbi]
        islot = sbi % 2
        inter_slot = inter2[islot]
        pend = []
        for ht0 in range(0, HT, 2):
            nh = min(2, HT - ht0)
            par = (ht0 // 2) % NWD
            ddv = pre.get(ht0) or wd_load(sbi, ht0)
            for pfpar, pht, pev in pend:
                kb.wait("gpsimd", "vS", pev)
                kb._pg_fbuf[pfpar] = kb.dma("gpsimd",
                            (lambda ht=pht, fpar=pfpar: lambda e: e.dma_start(
                                out=y_d[ht, :, :],
                                in_=fbuf[fpar][:, 0:OWN], accum_op=ALU.add))(),
                            f"dFA{pfpar}")
            pend = []
            for u in range(nh):
                ht = ht0 + u
                fpar = ht % 2
                kb.wait("vector", f"dFA{fpar}", kb._pg_fbuf[fpar])
                ev = 0
                for rb in range(NRB):
                    bk = (ht * NRB + rb) % 2
                    bank = ptrF[bk]
                    kb.wait("tensor", f"dWD{par}", ddv)
                    kb.wait("tensor", "vS", inter_done)
                    kb.wait("tensor", "vS", kb._pg_fbank[bk])
                    for itl in range(sbn):
                        kb.op("tensor",
                              (lambda itl=itl, rb=rb, bank=bank, u=u, par=par, inter_slot=inter_slot: lambda e: e.matmul(
                                  bank[:, 0:RB], wd[par][:, (u * SBmax + itl) * 128:(u * SBmax + itl + 1) * 128],
                                  inter_slot[:, itl, rb * RB:(rb + 1) * RB],
                                  start=(itl == 0), stop=(itl == sbn - 1)))(),
                              inc="tS" if itl == sbn - 1 else None)
                    kb._pg_wd_used[par] = kb.val("tS")
                    kb.wait("vector", "tS", kb.val("tS"))
                    ev = kb.op("vector",
                               (lambda bank=bank, fpar=fpar, rb=rb: lambda e: e.tensor_copy(
                                   fbuf[fpar][:, rb * RB:(rb + 1) * RB], bank[:, 0:RB]))(),
                               inc="vS")
                    kb._pg_fbank[bk] = ev
                pend.append((fpar, ht, ev))
        for pfpar, pht, pev in pend:
            kb.wait("gpsimd", "vS", pev)
            kb._pg_fbuf[pfpar] = kb.dma("gpsimd",
                        (lambda ht=pht, fpar=pfpar: lambda e: e.dma_start(
                            out=y_d[ht, :, :],
                            in_=fbuf[fpar][:, 0:OWN], accum_op=ALU.add))(),
                        f"dFA{pfpar}")
        kb._pg_islot[islot] = kb.val("tS")

    pend_sb = None
    for sbi in range(nsb2):
        pre = {}
        if pend_sb is not None:
            for ht0 in (0, 2):
                if ht0 < HT:
                    pre[ht0] = wd_load(pend_sb[0], ht0)
        idn = emit_gateup(sbi)
        if pend_sb is not None:
            emit_down(pend_sb[0], pend_sb[1], pre)
        pend_sb = (sbi, idn)
    emit_down(pend_sb[0], pend_sb[1], {})
    kb.barrier()
    if phases <= 6:
        return _finish(nc, kb, es)

    NPH = 3
    yv = [ARF[:, i * HT * 128:(i + 1) * HT * 128] for i in range(NPH)]
    outv = [ARF[:, (NPH + i) * HT * 128:(NPH + 1 + i) * HT * 128] for i in range(NPH)]
    xot = [U1F[:, 0:D], U1F[:, D:2 * D], ARF[:, 2 * NPH * HT * 128:2 * NPH * HT * 128 + D]]
    kb._ph_out_guard = {i: 0 for i in range(NPH)}
    kb._ph_rr = [0]
    kb._ph_yv = {i: 0 for i in range(NPH)}
    kb._ph_xot = {i: 0 for i in range(NPH)}
    for ot in range(OWN // 128):
        par = ot % NPH
        kb.wait("sync", "tS", kb._ph_yv[par])
        dy = kb.dma("sync",
                    (lambda ot=ot, par=par: lambda e: e.dma_start(
                        out=yv[par].rearrange("p (h r) -> p h r", h=HT),
                        in_=y_d.rearrange("h p r -> p h r")[:, :, ot * 128:(ot + 1) * 128]))(),
                    f"dYL{par}")
        kb.wait("sync", "vS", kb._ph_xot[par])
        dx = kb.dma("sync",
                    (lambda ot=ot, par=par: lambda e: e.dma_start(
                        out=xot[par][:, :], in_=x_own[ot * 128:(ot + 1) * 128, :]))(),
                    f"dXL{par}")
        for ht0 in range(0, HT, 4):
            slot6 = kb._ph_rr[0] % 6
            kb._ph_rr[0] += 1
            bank = pm[slot6] if slot6 < 4 else ptrF[slot6 - 4]
            kb.wait("tensor", f"dYL{par}", dy)
            bg = kb._ph_bank.get(slot6) if hasattr(kb, "_ph_bank") else None
            if bg:
                kb.wait("tensor", "vS", bg)
            for k in range(4):
                ht = ht0 + k
                kb.op("tensor",
                      (lambda ht=ht, k=k, bank=bank, par=par: lambda e: e.transpose(
                          bank[:, k * 128:(k + 1) * 128], yv[par][:, ht * 128:(ht + 1) * 128],
                          identf[:, :]))(),
                      inc="tS" if k == 3 else None)
            tv = kb.val("tS")
            kb.wait("vector", "tS", tv)
            kb.wait("vector", f"dXL{par}", dx)
            kb.wait("vector", f"dOUT{par}", kb._ph_out_guard[par])
            av = kb.op("vector",
                       (lambda ht0=ht0, bank=bank, par=par: lambda e: e.tensor_tensor(
                           outv[par][:, ht0 * 128:(ht0 + 4) * 128], bank[:, 0:512],
                           xot[par][:, ht0 * 128:(ht0 + 4) * 128], ALU.add))(),
                       inc="vS")
            if not hasattr(kb, "_ph_bank"):
                kb._ph_bank = {}
            kb._ph_bank[slot6] = av
        kb._ph_yv[par] = kb.val("tS")
        kb._ph_xot[par] = kb.val("vS")
        kb.wait("sync", "vS", av)
        kb._ph_out_guard[par] = kb.dma("sync",
               (lambda ot=ot, par=par: lambda e: e.dma_start(
                   out=out_d[ot * 128:(ot + 1) * 128, :], in_=outv[par][:, :]))(),
               f"dOUT{par}")
    kb.wait("sync", "dOUT0", kb.val("dOUT0"))
    kb.wait("sync", "dOUT1", kb.val("dOUT1"))
    kb.barrier()

    return _finish(nc, kb, es)


def _finish(nc, kb, es):
    semobjs = {}
    for name in kb.sem_names:
        semobjs[name] = es.enter_context(nc.semaphore(name))
    with es:
        with nc.Block() as block:
            block.sync(kb.runner("sync", semobjs))
            block.gpsimd(kb.runner("gpsimd", semobjs))
            block.tensor(kb.runner("tensor", semobjs))
            block.vector(kb.runner("vector", semobjs))
            block.scalar(kb.runner("scalar", semobjs))
    return nc


def rope_tables(cfg, positions):
    HD = 128
    invf = cfg["theta"] ** (-np.arange(0, HD, 2, dtype=np.float64) / HD)
    ang = np.asarray(positions, np.float64)[None, :] * invf[:, None]
    cos = np.cos(ang).astype(np.float32)
    sin = np.sin(ang).astype(np.float32)
    cosd = np.concatenate([cos, cos], 0)
    return cosd, (-sin).astype(np.float32), sin.astype(np.float32)


def perm_even_odd():
    return np.concatenate([np.arange(0, 128, 2), np.arange(1, 128, 2)])


def make_host_inputs(inputs, cfg):
    c = derive(cfg)
    D, S, SL, OWN = c["D"], c["S"], c["SL"], c["OWN"]
    NQH, NKVH = c["NQH"], c["NKVH"]
    HD = 128
    x = np.asarray(inputs["L0_attn_x"], np.float32)
    w_qkv = np.asarray(inputs["L0_attn_w_qkv"], np.float32)
    w_o = np.ascontiguousarray(np.asarray(inputs["L0_attn_w_o"], np.float32))
    w_g = np.asarray(inputs["L0_ffn_w_gate"], np.float32)
    w_u = np.asarray(inputs["L0_ffn_w_up"], np.float32)
    w_dn = np.asarray(inputs["L0_ffn_w_down"], np.float32)

    HT = c["HT"]; IT = c["IT"]
    NPASS = (NKVH + 1) // 2
    PW = min(2, NKVH) * 256
    perm = perm_even_odd()
    wq = w_qkv[:, :D].reshape(D, NQH, HD)[:, :, perm].reshape(D, D)
    wk = w_qkv[:, D:D + NKVH * HD].reshape(D, NKVH, HD)[:, :, perm]
    wv = w_qkv[:, D + NKVH * HD:].reshape(D, NKVH, HD)
    w_kv = np.concatenate([wk, wv], axis=2).reshape(D, NKVH * 256)
    w_q_t = np.ascontiguousarray(
        wq.reshape(HT, 128, NQH, 128).transpose(2, 1, 0, 3))
    w_kv_t = np.ascontiguousarray(
        w_kv.reshape(HT, 128, NPASS, PW).transpose(2, 1, 0, 3))
    w_o_t = np.ascontiguousarray(
        w_o.reshape(HT, 128, HT, 128).transpose(2, 1, 0, 3))
    w_g_t = w_g.reshape(IT, 128, HT, 128).transpose(0, 3, 2, 1)
    w_u_t = w_u.reshape(IT, 128, HT, 128).transpose(0, 3, 2, 1)
    w_gu_t = np.ascontiguousarray(
        np.concatenate([w_g_t.reshape(IT, 128, HT * 128),
                        w_u_t.reshape(IT, 128, HT * 128)], axis=2))
    w_d_t = np.ascontiguousarray(
        w_dn.reshape(HT, 128, IT, 128).transpose(0, 3, 2, 1))

    cosd, sneg, spos = rope_tables(cfg, np.arange(S))
    identb = np.eye(128, dtype=bfdt)
    identf = np.eye(128, dtype=np.float32)
    onesc = np.ones((128, 1), np.float32)
    onesr = np.ones((1, 128), np.float32)

    shared = dict(w_q=w_q_t, w_kv=w_kv_t,
                  w_o=w_o_t, w_guT=w_gu_t, w_dT=w_d_t,
                  identb=identb, identf=identf, onesc=onesc, onesr=onesr,
                  onesb=np.ones((128, 1), bfdt),
                  cosd=np.ascontiguousarray(cosd), sneg=np.ascontiguousarray(sneg),
                  spos=np.ascontiguousarray(spos))

    in_maps = []
    metas = []
    for core in range(8):
        b, rank = core // RANKS, core % RANKS
        pos = [snake_pos(c, rank, s) for s in range(SL)]
        rows = np.concatenate([np.arange(p * 128, (p + 1) * 128) for p in pos])
        x_own = np.ascontiguousarray(x[b][rows])
        cosq, snegq, sposq = rope_tables(cfg, rows)
        masks = np.zeros((SL * 4, 128, 512), dtype=bfdt)
        tloc = np.arange(128)[:, None]
        qloc = np.arange(128)[None, :]
        for s in range(SL):
            for dt in range(4):
                tt = c["T"][s] - 4 + dt
                m = ((tt * 128 + tloc) <= (pos[s] * 128 + qloc)).astype(np.float32)
                masks[s * 4 + dt] = np.tile(m, (1, 4)).astype(bfdt)
        im = dict(shared)
        im.update(x_own=x_own, x_full=np.ascontiguousarray(x[b]),
                  cosq=np.ascontiguousarray(cosq),
                  snegq=np.ascontiguousarray(snegq),
                  sposq=np.ascontiguousarray(sposq), masks=masks)
        in_maps.append(im)
        metas.append((b, rows))
    return in_maps, metas


def assemble_output(cfg, results, metas):
    c = derive(cfg)
    out = np.empty((cfg["B"], c["S"], c["D"]), np.float32)
    for core, ((b, rows), r) in enumerate(zip(metas, results)):
        out[b][rows] = np.asarray(r["out_own"], np.float32)
    return out


def kernel(L0_attn_x, L0_attn_w_qkv, L0_attn_w_o, L0_ffn_w_gate, L0_ffn_w_up,
           L0_ffn_w_down):
    cfg = full_cfg()
    inputs = dict(L0_attn_x=L0_attn_x, L0_attn_w_qkv=L0_attn_w_qkv,
                  L0_attn_w_o=L0_attn_w_o, L0_ffn_w_gate=L0_ffn_w_gate,
                  L0_ffn_w_up=L0_ffn_w_up, L0_ffn_w_down=L0_ffn_w_down)
    in_maps, metas = make_host_inputs(inputs, cfg)
    nc = build_program(cfg)
    res = run_bass_kernel_spmd(nc, in_maps, core_ids=list(range(8)))
    return assemble_output(cfg, res.results, metas)
